# version 09
# baseline (speedup 1.0000x reference)
import numpy as np
from contextlib import ExitStack

import concourse.bass as bass
import concourse.mybir as mybir
from concourse.bass_utils import run_bass_kernel_spmd

F32 = mybir.dt.float32
BF16 = mybir.dt.bfloat16
U8 = mybir.dt.uint8
AF = mybir.ActivationFunctionType
ALU = mybir.AluOpType

H = 2048
HC = H // 128
IFF = 8192
IC = IFF // 128
NCORES = 8
EPS = 1e-5



def _panelize(w):
    K, M = w.shape
    p = w.reshape(K // 128, 128, M // 128, 128).transpose(2, 1, 0, 3)
    return np.ascontiguousarray(p.reshape(M // 128, 128, K))


def _vec128(v):
    n = v.shape[0] // 128
    return np.ascontiguousarray(np.asarray(v).reshape(n, 128).T)


def _low_byte(w_int32):
    a = np.ascontiguousarray(np.asarray(w_int32))
    return np.ascontiguousarray(a.view(np.uint8).reshape(*a.shape, 4)[..., 0])


_L0_ORDER = [16 * j + gh for gh in range(16) for j in range(4)]


def _halves(pan):
    n, _, K = pan.shape
    return np.ascontiguousarray(
        pan.reshape(n, 128, 4, K // 4).transpose(0, 2, 1, 3).reshape(4 * n, 128, K // 4))


def host_prep(inp):
    w = {}
    w["wf"] = _panelize(np.asarray(inp["L0_attn_w_f"]))
    w["wi"] = _panelize(np.asarray(inp["L0_attn_w_i"]))
    w["wv"] = _panelize(np.asarray(inp["L0_attn_w_v"]))
    w["wq"] = _panelize(np.asarray(inp["L0_attn_w_q"]))
    w["wro"] = _panelize(np.asarray(inp["L0_attn_w_ro"]))
    w["wup"] = _panelize(np.asarray(inp["L0_attn_w_up"]))
    w["wgl"] = _panelize(np.asarray(inp["L0_attn_w_gate"]))
    w["wdnl"] = _panelize(np.asarray(inp["L0_attn_w_down"]))

    def perm_cols(m):
        return np.ascontiguousarray(m.reshape(m.shape[0], 2048, 4).transpose(0, 2, 1).reshape(m.shape[0], 8192))

    def perm_rows(m):
        return np.ascontiguousarray(m.reshape(2048, 4, m.shape[1]).transpose(1, 0, 2).reshape(8192, m.shape[1]))

    fg = _panelize(perm_cols(np.asarray(inp["L0_ffn_w_gate"])))
    fu = _panelize(perm_cols(np.asarray(inp["L0_ffn_w_up"])))
    w["f0g"] = np.ascontiguousarray(fg[_L0_ORDER])
    w["f0u"] = np.ascontiguousarray(fu[_L0_ORDER])
    w["f0d"] = _halves(_panelize(perm_rows(np.asarray(inp["L0_ffn_w_down"]))))

    w["cT"] = _panelize(np.ascontiguousarray(np.asarray(inp["L1_attn_C_w"]).T))
    w["dT"] = _panelize(np.ascontiguousarray(np.asarray(inp["L1_attn_D_w"]).T))

    w["f1g"] = _panelize(np.ascontiguousarray(_low_byte(inp["L1_ffn_gate_w"]).T))
    w["f1u"] = _panelize(np.ascontiguousarray(_low_byte(inp["L1_ffn_up_w"]).T))
    w["f1d"] = _halves(_panelize(np.ascontiguousarray(_low_byte(inp["L1_ffn_down_w"]).T)))

    w["cdb"] = np.ascontiguousarray(
        np.concatenate([_vec128(inp["L1_attn_C_b"]), _vec128(inp["L1_attn_D_b"])], axis=1))
    w["logd"] = _vec128(inp["L1_attn_log_delta"])
    w["gsc"] = _vec128(inp["L1_ffn_gate_scale"])
    w["usc"] = _vec128(inp["L1_ffn_up_scale"])
    w["dsc"] = _vec128(inp["L1_ffn_down_scale"])
    w["inist"] = _vec128(inp["L0_attn_init_state"])
    w["amat"] = np.ascontiguousarray(
        np.asarray(inp["L1_attn_A"]).reshape(16, 128, 64).transpose(1, 0, 2).reshape(128, 1024))
    w["bmat"] = np.ascontiguousarray(
        np.asarray(inp["L1_attn_B"]).reshape(16, 128, 64).transpose(1, 0, 2).reshape(128, 1024))
    return w




class Build:
    N_WLANES = 8

    def __init__(self, nc, sems):
        self.nc = nc
        self.sems = sems
        self.ops = {e: [] for e in ("tensor", "vector", "scalar", "gpsimd", "sync")}
        self.cnt = {k: 0 for k in sems}
        self.last_wait = {e: {} for e in self.ops}
        self.wlane_ptr = 0

    def emit(self, eng, fn):
        self.ops[eng].append(fn)

    def wait(self, eng, sem, target):
        if target <= 0 or self.last_wait[eng].get(sem, 0) >= target:
            return
        self.last_wait[eng][sem] = target
        mult = 16 if (sem == "dx" or sem.startswith("dw") or sem.startswith("dy")) else 1
        s = self.sems[sem]
        self.emit(eng, lambda e, s=s, t=target * mult: e.wait_ge(s, t))

    def op(self, eng, fn, inc=None):
        if inc is None:
            self.emit(eng, fn)
            return None
        self.cnt[inc] += 1
        amt = 16 if (inc == "dx" or inc.startswith("dw") or inc.startswith("dy")) else 1
        s = self.sems[inc]
        self.emit(eng, lambda e, f=fn, s=s, a=amt: f(e).then_inc(s, a))
        return self.cnt[inc]

    def wdma(self, fn):
        lane = f"dw{self.wlane_ptr}"
        self.wlane_ptr = (self.wlane_ptr + 1) % self.N_WLANES
        self.wait("gpsimd", lane, self.cnt[lane])
        t = self.op("gpsimd", fn, inc=lane)
        return (lane, t)

    def cdma(self, fn, pre_wait=None):
        lane = f"dy{getattr(self, 'ylane_ptr', 0)}"
        self.ylane_ptr = (getattr(self, 'ylane_ptr', 0) + 1) % 4
        self.wait("sync", lane, self.cnt[lane])
        if pre_wait is not None:
            self.wait("sync", *pre_wait)
        t = self.op("sync", fn, inc=lane)
        return (lane, t)

    def sdma(self, fn):
        self.wait("sync", "dx", self.cnt["dx"])
        return self.op("sync", fn, inc="dx")

    def drain(self, eng):
        self.emit(eng, lambda e: e.drain())


class Banks:
    def __init__(self, b, tensors):
        self.b = b
        self.t = tensors
        self.state = [("a", 0)] * len(tensors)
        self.held = [False] * len(tensors)
        self.ptr = 0

    def alloc(self):
        for _ in range(len(self.t)):
            i = self.ptr
            self.ptr = (self.ptr + 1) % len(self.t)
            if not self.held[i]:
                sem, tgt = self.state[i]
                self.b.wait("tensor", sem, tgt)
                self.held[i] = True
                return i
        raise RuntimeError("psum banks exhausted")

    def release(self, i, sem, target):
        self.held[i] = False
        self.state[i] = (sem, target)


class Ring:
    def __init__(self, b, tensor, n):
        self.b = b
        self.t = tensor
        self.n = n
        self.free_at = [dict() for _ in range(n)]
        self.ptr = 0

    def alloc(self, k):
        if self.ptr + k > self.n:
            self.ptr = 0
        base = self.ptr
        self.ptr += k
        req = {}
        for s in range(base, base + k):
            for sem, tgt in self.free_at[s].items():
                req[sem] = max(req.get(sem, 0), tgt)
        return base, req

    def mark(self, base, k, sem, target):
        for s in range(base, base + k):
            d = self.free_at[s]
            d[sem] = max(d.get(sem, 0), target)


class PanelStream:
    def __init__(self, b, ring, dram_ap, n_units, unit_elems, slots_per_unit, group,
                 scratch=None, convert=False):
        self.b = b
        self.ring = ring
        self.dram = dram_ap
        self.scratch = scratch
        self.convert = convert
        self.n_units = n_units
        self.ue = unit_elems
        self.spu = slots_per_unit
        self.group = group
        self.next_fetch = 0
        self.inflight = {}

    def ensure(self, unit):
        while self.next_fetch <= unit:
            u0 = self.next_fetch
            g = min(self.group, self.n_units - u0)
            base, req = self.ring.alloc(g * self.spu)
            for sem, tgt in req.items():
                self.b.wait("gpsimd", sem, tgt)
            if g == 1:
                src = self.dram[u0]
                dst = self.ring.t[:, base * 2048: base * 2048 + self.ue]
            else:
                src = self.dram[u0:u0 + g].transpose([1, 0, 2])
                dst = self.ring.t[:, base * 2048: base * 2048 + g * self.ue].rearrange(
                    "p (g n) -> p g n", g=g)
            tok = self.b.wdma(lambda e, d=dst, s=src: e.dma_start(d, s))
            if self.convert:
                if g == 1:
                    wsrc = self.ring.t[:, base * 2048: base * 2048 + self.ue]
                    wdst = self.scratch[u0]
                else:
                    wsrc = self.ring.t[:, base * 2048: base * 2048 + g * self.ue].rearrange(
                        "p (g n) -> p g n", g=g)
                    wdst = self.scratch[u0:u0 + g].transpose([1, 0, 2])
                wtok = self.b.cdma(lambda e, d=wdst, s=wsrc: e.dma_start(d, s), pre_wait=tok)
                self.ring.mark(base, g * self.spu, wtok[0], wtok[1])
            for i in range(g):
                self.inflight[u0 + i] = (base + i * self.spu, tok)
            self.next_fetch += g

    def unit(self, u):
        self.ensure(u)
        return self.inflight[u]

    def lhsT(self, slot_base, k):
        return self.ring.t[:, slot_base * 2048 + k * 128: slot_base * 2048 + (k + 1) * 128]

    def done(self, u, pe_target):
        base, _ = self.inflight.pop(u)
        self.ring.mark(base, self.spu, "pe", pe_target)




def build_nc(S=2048, T=512):
    n_tiles = S // T
    TS2 = T // 2
    nc = bass.Bass("TRN2", target_bir_lowering=False)

    xT = nc.dram_tensor("xT", [H, S], F32, kind="ExternalInput").ap()
    yT = nc.dram_tensor("yT", [H, S], F32, kind="ExternalOutput").ap()
    dr = {}
    for nm, units, ue, dt in [
        ("wf", 16, 2048, F32), ("wi", 16, 2048, F32), ("wv", 16, 2048, F32),
        ("wq", 16, 2048, F32), ("wro", 16, 2048, F32), ("wup", 16, 2048, F32),
        ("wgl", 16, 2048, F32), ("wdnl", 16, 2048, F32),
        ("f0g", 64, 2048, F32), ("f0u", 64, 2048, F32), ("f0d", 64, 2048, F32),
        ("cT", 16, 2048, F32), ("dT", 16, 2048, F32),
        ("f1g", 64, 2048, U8), ("f1u", 64, 2048, U8), ("f1d", 64, 2048, U8),
    ]:
        dr[nm] = nc.dram_tensor(nm, [units, 128, ue], dt, kind="ExternalInput").ap()
    for nm, cols in [("cdb", 32), ("logd", 16), ("gsc", 64), ("usc", 64),
                     ("dsc", 16), ("inist", 16)]:
        dr[nm] = nc.dram_tensor(nm, [128, cols], F32, kind="ExternalInput").ap()
    dr["amat"] = nc.dram_tensor("amat", [128, 1024], F32, kind="ExternalInput").ap()
    dr["bmat"] = nc.dram_tensor("bmat", [128, 1024], F32, kind="ExternalInput").ap()
    FP32_FAMS = ("wf", "wi", "wv", "wq", "wro", "wup", "wgl", "wdnl",
                 "f0g", "f0u", "f0d", "cT", "dT")
    scr = {}
    for nm, units, ue, dt in [
        ("wf", 16, 2048, F32), ("wi", 16, 2048, F32), ("wv", 16, 2048, F32),
        ("wq", 16, 2048, F32), ("wro", 16, 2048, F32), ("wup", 16, 2048, F32),
        ("wgl", 16, 2048, F32), ("wdnl", 16, 2048, F32),
        ("f0g", 64, 2048, F32), ("f0u", 64, 2048, F32), ("f0d", 64, 2048, F32),
        ("cT", 16, 2048, F32), ("dT", 16, 2048, F32),
    ]:
        scr[nm] = nc.dram_tensor("scr_" + nm, [units, 128, ue], BF16, kind="Internal").ap()

    es = ExitStack()
    sb = lambda nm, shape, dt: es.enter_context(nc.sbuf_tensor(nm, shape, dt))

    xb = sb("xb", [128, HC * T], F32)
    xn = sb("xn", [128, HC * T], BF16)
    bufA = sb("bufA", [128, 2 * HC * T], BF16)
    bufB = sb("bufB", [128, 2 * HC * T], BF16)
    ro = sb("ro", [128, HC * T], BF16)
    loc = sb("loc", [128, HC * T], BF16)
    ring_t = sb("ring", [128, 8 * 2048], BF16)
    h32 = sb("h32", [128, 4 * T], F32)
    sp = sb("sp", [128, 8 * TS2], F32)
    tg = sb("tg", [128, 2 * T], F32)
    tg2 = sb("tg2", [128, 2 * T], F32)
    stat_bf = sb("stat_bf", [1, 2 * T], BF16)
    ones_col = sb("ones_col", [128, 1], BF16)
    ones_row = sb("ones_row", [1, 128], BF16)
    n128_row = sb("n128_row", [1, 128], BF16)
    cdb = sb("cdb_s", [128, 32], F32)
    ascl = sb("ascl", [128, 16], F32)
    abr = sb("abr", [128, 16], F32)
    gsc = sb("gsc_s", [128, 64], F32)
    usc = sb("usc_s", [128, 64], F32)
    dsc = sb("dsc_s", [128, 16], F32)
    st0 = sb("st0", [128, 16], F32)
    st1 = sb("st1", [128, 16], F32)

    bufA32 = bufA[:, :].bitcast(F32)
    bufB32 = bufB[:, :].bitcast(F32)

    pb = [es.enter_context(nc.psum_tensor(f"pb{k}", [128, T], F32)) for k in range(8)]

    sems = {nm: es.enter_context(nc.semaphore("sem_" + nm))
            for nm in (["dx", "pe", "v", "a", "g"] + [f"dw{i}" for i in range(Build.N_WLANES)]
                       + [f"dy{i}" for i in range(4)])}

    b = Build(nc, sems)
    banks = Banks(b, pb)
    ring = Ring(b, ring_t, 8)

    def A32(c, lo=0, w=T):
        return bufA32[:, c * T + lo: c * T + lo + w]

    def B32(c, lo=0, w=T):
        return bufB32[:, c * T + lo: c * T + lo + w]

    def XB(c, lo=0, w=T):
        return xb[:, c * T + lo: c * T + lo + w]

    def XN(c, lo=0, w=T):
        return xn[:, c * T + lo: c * T + lo + w]

    def h_ap(c, lo=0, w=T):
        if c < 32:
            t, off = bufB, c * T
        elif c < 48:
            t, off = ro, (c - 32) * T
        else:
            t, off = loc, (c - 48) * T
        return t[:, off + lo: off + lo + w]

    st = {}

    def prologue():
        b.op("gpsimd", lambda e: e.memset(ones_col[:, :], 1.0), inc="g")
        b.op("gpsimd", lambda e: e.memset(ones_row[:, :], 1.0), inc="g")
        b.op("gpsimd", lambda e: e.memset(n128_row[:, :], -128.0), inc="g")
        for nm, dst in [("cdb", cdb), ("gsc", gsc), ("usc", usc), ("dsc", dsc),
                        ("inist", st0), ("logd", ascl)]:
            b.sdma(lambda e, d=dst, s=dr[nm]: e.dma_start(d[:, :], s))
        b.sdma(lambda e: e.dma_start(h32[:, 0:1024], dr["amat"]))
        pro_dx = b.sdma(lambda e: e.dma_start(h32[:, 1024:2048], dr["bmat"]))
        st["pro_dx"] = pro_dx
        b.wait("vector", "dx", pro_dx)
        b.wait("scalar", "dx", pro_dx)
        b.op("vector", lambda e: e.memset(st1[:, :], 0.0))
        b.op("vector", lambda e: e.tensor_mul(sp[:, 0:1024], h32[:, 0:1024], h32[:, 1024:2048]))
        b.drain("vector")
        for c in range(16):
            b.op("vector", lambda e, c=c: e.tensor_reduce(
                abr[:, c:c + 1], sp[:, c * 64:(c + 1) * 64], mybir.AxisListType.X, ALU.add))
        b.drain("vector")
        tv = b.op("vector", lambda e: e.tensor_single_scalar(sp[:, 16:32], ascl[:, :], 0.0, ALU.max), inc="v")
        b.wait("scalar", "v", tv)
        b.op("scalar", lambda e: e.activation(sp[:, 0:16], ascl[:, :], AF.Abs))
        b.drain("scalar")
        b.op("scalar", lambda e: e.activation(sp[:, 32:48], sp[:, 0:16], AF.Exp, scale=-1.0))
        b.drain("scalar")
        ta = b.op("scalar", lambda e: e.activation(sp[:, 48:64], sp[:, 32:48], AF.Ln, bias=1.0), inc="a")
        b.wait("vector", "a", ta)
        tv = b.op("vector", lambda e: e.tensor_add(sp[:, 64:80], sp[:, 16:32], sp[:, 48:64]), inc="v")
        b.wait("scalar", "v", tv)
        st["prologue_a"] = b.op("scalar", lambda e: e.activation(ascl[:, :], sp[:, 64:80], AF.Exp, scale=-1.0), inc="a")

    def load_x(tile):
        b.wait("sync", "v", st.get("combine_v", 0))
        b.wait("sync", "dx", st.get("out_dx", 0))
        src = xT.rearrange("(c p) s -> p c s", p=128)[:, :, tile * T:(tile + 1) * T]
        dst = xb[:, :].rearrange("p (c t) -> p c t", c=HC)
        st["xb_dx"] = b.sdma(lambda e, d=dst, s=src: e.dma_start(d, s))

    def layernorm(layer):
        pre = st.pop("ln_pre", None)
        b.wait("scalar", "dx", st["xb_dx"])
        b.wait("scalar", "v", st.get("combine_v", 0))
        if pre is None:
            b.wait("scalar", "pe", st.get("xn_free_pe", 0))
            b.op("scalar", lambda e: e.activation(bufA[:, 0:HC * T], xb[:, :], AF.Copy))
            ta_sq = b.op("scalar", lambda e: e.activation(xn[:, :], xb[:, :], AF.Square), inc="a")
            bm = banks.alloc()
            bs = banks.alloc()
            b.wait("tensor", "a", ta_sq)
            b.wait("tensor", "g", 3)
            for c in range(HC):
                b.op("tensor", lambda e, c=c, k=bm: e.matmul(
                    pb[k][0:1, :], ones_col[:, :], bufA[:, c * T:(c + 1) * T],
                    start=(c == 0), stop=(c == HC - 1)),
                    inc="pe" if c == HC - 1 else None)
            for c in range(HC):
                b.op("tensor", lambda e, c=c, k=bs: e.matmul(
                    pb[k][0:1, :], ones_col[:, :], XN(c),
                    start=(c == 0), stop=(c == HC - 1)),
                    inc="pe" if c == HC - 1 else None)
            t_ps = b.cnt["pe"]
        else:
            bm, bs, t_ps = pre
        SL = lambda i: sp[0:1, i * T:(i + 1) * T]
        b.wait("vector", "pe", t_ps)
        b.op("vector", lambda e: e.tensor_single_scalar(SL(0), pb[bm][0:1, :], 1.0 / H, ALU.mult))
        tv = b.op("vector", lambda e: e.tensor_single_scalar(SL(1), pb[bs][0:1, :], 1.0 / H, ALU.mult), inc="v")
        banks.release(bm, "v", tv)
        banks.release(bs, "v", tv)
        b.wait("scalar", "v", tv)
        ta = b.op("scalar", lambda e: e.activation(SL(2), SL(0), AF.Square), inc="a")
        b.wait("vector", "a", ta)
        b.drain("vector")
        b.op("vector", lambda e: e.tensor_sub(SL(1), SL(1), SL(2)))
        b.drain("vector")
        tv = b.op("vector", lambda e: e.tensor_single_scalar(SL(1), SL(1), EPS, ALU.add), inc="v")
        b.wait("scalar", "v", tv)
        ta = b.op("scalar", lambda e: e.activation(SL(2), SL(1), AF.Sqrt), inc="a")
        b.wait("vector", "a", ta)
        b.drain("vector")
        b.op("vector", lambda e: e.reciprocal(SL(1), SL(2)))
        b.drain("vector")
        tv = b.op("vector", lambda e: e.scalar_tensor_tensor(SL(3), SL(0), -1.0, SL(1), ALU.mult, ALU.mult), inc="v")
        b.wait("scalar", "v", tv)
        b.op("scalar", lambda e: e.activation(stat_bf[0:1, 0:T], SL(1), AF.Copy))
        ta = b.op("scalar", lambda e: e.activation(stat_bf[0:1, T:2 * T], SL(3), AF.Copy), inc="a")
        ba = banks.alloc()
        bb = banks.alloc()
        b.wait("tensor", "a", ta)
        b.op("tensor", lambda e, k=ba: e.matmul(pb[k][:, :], ones_row[:, :], stat_bf[0:1, 0:T],
                                                start=True, stop=True))
        t_bc = b.op("tensor", lambda e, k=bb: e.matmul(pb[k][:, :], ones_row[:, :], stat_bf[0:1, T:2 * T],
                                                       start=True, stop=True), inc="pe")
        b.wait("vector", "pe", t_bc)
        b.wait("vector", "dx", st["xb_dx"])
        b.wait("vector", "v", st.get("combine_v", 0))
        for c in range(HC):
            b.op("vector", lambda e, c=c, k=ba: e.tensor_mul(B32(c), XB(c), pb[k][:, :]))
        b.drain("vector")
        for c in range(HC):
            last = (c == HC - 1)
            b.op("vector", lambda e, c=c, k=bb: e.tensor_add(XN(c), B32(c), pb[k][:, :]),
                 inc="v" if last else None)
        b.drain("vector")
        st["xn_v"] = b.cnt["v"]
        banks.release(ba, "v", st["xn_v"])
        banks.release(bb, "v", st["xn_v"])

    def mm_family(stream, u, bank, rhs_fn, n_k, w=T, extra=None, consume=True):
        base, dw_tok = stream.unit(u)
        b.wait("tensor", *dw_tok)
        b.wait("tensor", "v", st.get("xn_v", 0))
        out = pb[bank][:, 0:w]
        for k in range(n_k):
            last = (k == n_k - 1 and extra is None)
            b.op("tensor", lambda e, k=k, out=out, base=base, last=last: e.matmul(
                out, stream.lhsT(base, k), rhs_fn(k), start=(k == 0), stop=last),
                inc="pe" if last else None)
        if extra is not None:
            b.op("tensor", lambda e, out=out: e.matmul(out, extra[0], extra[1],
                                                       start=False, stop=True), inc="pe")
        t = b.cnt["pe"]
        if consume:
            stream.done(u, t)
        return t

    def fiv(streams):
        s_f, s_i, s_v = streams["wf"], streams["wi"], streams["wv"]
        for m in range(HC):
            k = banks.alloc()
            t = mm_family(s_f, m, k, lambda kk: XN(kk), HC)
            b.wait("scalar", "pe", t)
            ta = b.op("scalar", lambda e, k=k, m=m: e.activation(A32(m), pb[k][:, :], AF.Tanh), inc="a")
            banks.release(k, "a", ta)
        st["f_a"] = ta
        gp = {}
        for m in range(HC):
            ki = banks.alloc()
            ti = mm_family(s_i, m, ki, lambda kk: XN(kk), HC)
            kv = banks.alloc()
            tvm = mm_family(s_v, m, kv, lambda kk: XN(kk), HC)
            sl = (m % 2) * T
            b.wait("scalar", "v", gp.get(m - 2, 0))
            b.wait("scalar", "pe", ti)
            b.op("scalar", lambda e, k=ki, sl=sl: e.activation(tg[:, sl:sl + T], pb[k][:, :], AF.Sigmoid))
            b.wait("scalar", "pe", tvm)
            ta = b.op("scalar", lambda e, k=kv, sl=sl: e.activation(tg2[:, sl:sl + T], pb[k][:, :], AF.Silu), inc="a")
            banks.release(ki, "a", ta)
            banks.release(kv, "a", ta)
            b.wait("vector", "a", ta)
            gp[m] = b.op("vector", lambda e, m=m, sl=sl: e.tensor_mul(B32(m), tg[:, sl:sl + T], tg2[:, sl:sl + T]), inc="v")
        st["g_v"] = gp[HC - 1]
        st["fiv_a"] = ta

    def scan0():
        b.wait("vector", "a", st["f_a"])
        b.wait("vector", "dx", st["pro_dx"])
        b.drain("vector")
        for c in range(HC):
            b.op("vector", lambda e, c=c: e.tensor_tensor_scan(
                B32(c), A32(c), B32(c), st0[:, c:c + 1], ALU.mult, ALU.add))
        b.drain("vector")
        for c in range(HC):
            last = (c == HC - 1)
            b.op("vector", lambda e, c=c: e.tensor_copy(st0[:, c:c + 1], B32(c, T - 1, 1)),
                 inc="v" if last else None)
        st["scan_v"] = b.cnt["v"]
        b.drain("vector")

    def local(streams):
        s_u, s_g = streams["wup"], streams["wgl"]
        b.wait("scalar", "v", st["g_v"])
        lv = {}
        for m in range(HC):
            ku = banks.alloc()
            tu = mm_family(s_u, m, ku, lambda kk: XN(kk), HC)
            kg = banks.alloc()
            tgm = mm_family(s_g, m, kg, lambda kk: XN(kk), HC)
            sl = (m % 2) * T
            b.wait("scalar", "pe", tgm)
            b.wait("scalar", "v", lv.get(m - 2, 0))
            ta = b.op("scalar", lambda e, k=kg, sl=sl: e.activation(tg2[:, sl:sl + T], pb[k][:, :], AF.Silu), inc="a")
            banks.release(kg, "a", ta)
            b.wait("vector", "a", ta)
            b.wait("vector", "pe", tu)
            lv[m] = b.op("vector", lambda e, k=ku, m=m, sl=sl: e.tensor_mul(
                loc[:, m * T:(m + 1) * T], pb[k][:, :], tg2[:, sl:sl + T]), inc="v")
            banks.release(ku, "v", lv[m])
        st["loc_done_v"] = lv[HC - 1]

    def q_readout(streams):
        s_q = streams["wq"]
        b.wait("vector", "a", st["fiv_a"])
        ra = {}
        for m in range(HC):
            k = banks.alloc()
            t = mm_family(s_q, m, k, lambda kk: XN(kk), HC)
            sl = (m % 2) * T
            b.wait("vector", "pe", t)
            b.wait("vector", "v", st["scan_v"])
            b.wait("vector", "a", ra.get(m - 2, 0))
            tvp = b.op("vector", lambda e, k=k, m=m, sl=sl: e.tensor_mul(tg[:, sl:sl + T], pb[k][:, :], B32(m)), inc="v")
            banks.release(k, "v", tvp)
            b.wait("scalar", "v", tvp)
            ra[m] = b.op("scalar", lambda e, m=m, sl=sl: e.activation(
                ro[:, m * T:(m + 1) * T], tg[:, sl:sl + T], AF.Silu), inc="a")
        st["ro_done_a"] = ra[HC - 1]
        st["q_v"] = b.cnt["v"]

    def saccum(streams):
        s_ro, s_dn = streams["wro"], streams["wdnl"]
        for m in range(HC):
            k = banks.alloc()
            base_r, dw_r = s_ro.unit(m)
            base_d, dw_d = s_dn.unit(m)
            b.wait("tensor", *dw_r)
            b.wait("tensor", *dw_d)
            b.wait("tensor", "a", st["ro_done_a"])
            b.wait("tensor", "v", st["loc_done_v"])
            for kk in range(HC):
                b.op("tensor", lambda e, kk=kk, base=base_r, k=k: e.matmul(
                    pb[k][:, :], s_ro.lhsT(base, kk), ro[:, kk * T:(kk + 1) * T],
                    start=(kk == 0), stop=False))
            for kk in range(HC):
                b.op("tensor", lambda e, kk=kk, base=base_d, k=k: e.matmul(
                    pb[k][:, :], s_dn.lhsT(base, kk), loc[:, kk * T:(kk + 1) * T],
                    start=False, stop=(kk == HC - 1)),
                    inc="pe" if kk == HC - 1 else None)
            t = b.cnt["pe"]
            s_ro.done(m, t)
            s_dn.done(m, t)
            b.wait("scalar", "pe", t)
            b.wait("scalar", "v", st["scan_v"])
            ta = b.op("scalar", lambda e, k=k, m=m: e.activation(A32(m), pb[k][:, :], AF.Copy), inc="a")
            banks.release(k, "a", ta)
        st["s_sb_a"] = ta
        st["h_free_pe"] = t

    def zrow(slot):
        k = banks.alloc()
        b.wait("tensor", "v", st["xn_v"])
        for c in range(HC):
            b.op("tensor", lambda e, c=c, k=k: e.matmul(
                pb[k][0:1, :], ones_col[:, :], XN(c), start=(c == 0), stop=(c == HC - 1)),
                inc="pe" if c == HC - 1 else None)
        t = b.cnt["pe"]
        b.wait("scalar", "pe", t)
        ta = b.op("scalar", lambda e, k=k, slot=slot: e.activation(
            stat_bf[0:1, slot * T:(slot + 1) * T], pb[k][0:1, :], AF.Copy), inc="a")
        banks.release(k, "a", ta)
        return ta

    def sparsify(gh, t_h, inc_last=False):
        A_ = lambda j: sp[:, j * TS2:(j + 1) * TS2]
        T_ = lambda k: sp[:, (4 + k) * TS2:(5 + k) * TS2]
        H_ = lambda j, s: h32[:, j * T + s * TS2: j * T + s * TS2 + TS2]
        out_t = None
        for s in (0, 1):
            b.drain("vector")
            for j in range(4):
                b.op("vector", lambda e, j=j, s=s: e.tensor_single_scalar(
                    A_(j), H_(j, s), -1.0, ALU.mult))
            b.drain("vector")
            for j in range(4):
                b.op("vector", lambda e, j=j, s=s: e.tensor_tensor(
                    A_(j), A_(j), H_(j, s), ALU.max))
            b.drain("vector")
            b.op("vector", lambda e: e.tensor_max(T_(0), A_(0), A_(1)))
            b.op("vector", lambda e: e.tensor_tensor(T_(1), A_(0), A_(1), ALU.min))
            b.op("vector", lambda e: e.tensor_max(T_(2), A_(2), A_(3)))
            b.op("vector", lambda e: e.tensor_tensor(T_(3), A_(2), A_(3), ALU.min))
            b.drain("vector")
            b.op("vector", lambda e: e.tensor_tensor(T_(0), T_(0), T_(2), ALU.min))
            b.op("vector", lambda e: e.tensor_max(T_(1), T_(1), T_(3)))
            b.drain("vector")
            b.op("vector", lambda e: e.tensor_max(T_(0), T_(0), T_(1)))
            b.drain("vector")
            for j in range(4):
                b.op("vector", lambda e, j=j: e.tensor_tensor(A_(j), A_(j), T_(0), ALU.is_ge))
            b.drain("vector")
            for j in range(4):
                c = 16 * j + gh
                last = (s == 1 and j == 3)
                out_t = b.op("vector", lambda e, j=j, s=s, c=c: e.tensor_mul(
                    h_ap(c, s * TS2, TS2), A_(j), H_(j, s)), inc="v" if last else None)
        st["sp_free_v"] = out_t
        b.drain("vector")
        return out_t

    def ffnA(layer, streams):
        quant = (layer == 1)
        s_g = streams["f1g" if quant else "f0g"]
        s_u = streams["f1u" if quant else "f0u"]
        if quant:
            z_a = zrow(0)
            b.wait("scalar", "v", st["cd_v"])
        else:
            b.wait("scalar", "v", st["q_v"])
        order = list(range(IC)) if quant else _L0_ORDER
        b.wait("vector", "pe", st.get("h_free_pe", 0))
        b.drain("vector")
        hp = {}
        for idx, m in enumerate(order):
            for sub in (0, 1):
                lo = sub * TS2
                kg = banks.alloc()
                extra = None
                if quant:
                    b.wait("tensor", "a", z_a)
                    extra = (n128_row[:, :], stat_bf[0:1, lo: lo + TS2])
                tg_t = mm_family(s_g, idx, kg, lambda kk, lo=lo: XN(kk, lo, TS2),
                                 HC, w=TS2, extra=extra, consume=(sub == 1))
                ku = banks.alloc()
                tu_t = mm_family(s_u, idx, ku, lambda kk, lo=lo: XN(kk, lo, TS2),
                                 HC, w=TS2, extra=extra, consume=(sub == 1))
                sl = (idx % 2) * T + sub * TS2
                b.wait("scalar", "pe", tg_t)
                b.wait("scalar", "v", hp.get((idx - 2, sub), 0))
                if quant:
                    ta = b.op("scalar", lambda e, k=kg, m=m, sl=sl: e.activation(
                        tg[:, sl:sl + TS2], pb[k][:, 0:TS2], AF.Silu, scale=gsc[:, m:m + 1]), inc="a")
                else:
                    ta = b.op("scalar", lambda e, k=kg, sl=sl: e.activation(
                        tg[:, sl:sl + TS2], pb[k][:, 0:TS2], AF.Silu), inc="a")
                banks.release(kg, "a", ta)
                b.wait("vector", "a", ta)
                b.wait("vector", "pe", tu_t)
                if quant:
                    hp[(idx, sub)] = b.op("vector", lambda e, k=ku, m=m, sl=sl, lo=lo: e.scalar_tensor_tensor(
                        h_ap(m, lo, TS2), pb[k][:, 0:TS2], usc[:, m:m + 1], tg[:, sl:sl + TS2],
                        ALU.mult, ALU.mult), inc="v")
                else:
                    j = m // 16
                    hp[(idx, sub)] = b.op("vector", lambda e, k=ku, j=j, sl=sl, lo=lo: e.tensor_mul(
                        h32[:, j * T + lo: j * T + lo + TS2], pb[k][:, 0:TS2], tg[:, sl:sl + TS2]), inc="v")
                banks.release(ku, "v", hp[(idx, sub)])
            if not quant and m >= 48:
                tvh = sparsify(m - 48, hp[(idx, 1)], inc_last=(m == 63))
        st["h_v"] = tvh if not quant else hp[(IC - 1, 1)]
        st["xn_last_pe"] = tu_t

    def ffnB(layer, streams, tile=0):
        quant = (layer == 1)
        stat = (layer == 0) or (tile + 1 < n_tiles)
        xtok = None
        if quant and stat:
            b.wait("gpsimd", "pe", st["xn_last_pe"])
            srcx = xT.rearrange("(c p) s -> p c s", p=128)[:, :, (tile + 1) * T:(tile + 2) * T]
            dstx = xn[:, :].rearrange("p (c t) -> p c t", c=HC)
            xtok = b.wdma(lambda e, d=dstx, s=srcx: e.dma_start(d, s))
        s_d = streams["f1d" if quant else "f0d"]
        zh_a = 0
        if quant:
            k = banks.alloc()
            b.wait("tensor", "v", st["h_v"])
            for c in range(IC):
                b.op("tensor", lambda e, c=c, k=k: e.matmul(
                    pb[k][0:1, :], ones_col[:, :], h_ap(c), start=(c == 0), stop=(c == IC - 1)),
                    inc="pe" if c == IC - 1 else None)
            t = b.cnt["pe"]
            b.wait("scalar", "pe", t)
            zh_a = b.op("scalar", lambda e, k=k: e.activation(
                stat_bf[0:1, T:2 * T], pb[k][0:1, :], AF.Copy), inc="a")
            banks.release(k, "a", zh_a)
        b.wait("scalar", "v", st["h_v"])
        cb = {}
        for m in range(HC):
            qu = [s_d.unit(4 * m + q) for q in range(4)]
            for sub in (0, 1):
                lo = sub * TS2
                k = banks.alloc()
                for _, tok in qu:
                    b.wait("tensor", *tok)
                b.wait("tensor", "v", st["h_v"])
                if quant:
                    b.wait("tensor", "a", zh_a)
                for kk in range(IC):
                    base = qu[kk // 16][0]
                    ko = kk % 16
                    last = (kk == IC - 1 and not quant)
                    b.op("tensor", lambda e, kk=kk, base=base, ko=ko, k=k, lo=lo, last=last: e.matmul(
                        pb[k][:, 0:TS2],
                        ring_t[:, base * 2048 + ko * 128: base * 2048 + (ko + 1) * 128],
                        h_ap(kk, lo, TS2), start=(kk == 0), stop=last),
                        inc="pe" if last else None)
                if quant:
                    b.op("tensor", lambda e, k=k, lo=lo: e.matmul(
                        pb[k][:, 0:TS2], n128_row[:, :],
                        stat_bf[0:1, T + lo:T + lo + TS2],
                        start=False, stop=True), inc="pe")
                t = b.cnt["pe"]
                sl = (m % 2) * T + lo
                b.wait("scalar", "pe", t)
                b.wait("scalar", "v", cb.get((m - 2, sub), 0))
                if quant:
                    ta = b.op("scalar", lambda e, k=k, m=m, sl=sl: e.activation(
                        tg[:, sl:sl + TS2], pb[k][:, 0:TS2], AF.Silu, scale=dsc[:, m:m + 1]), inc="a")
                else:
                    ta = b.op("scalar", lambda e, k=k, sl=sl: e.activation(
                        tg[:, sl:sl + TS2], pb[k][:, 0:TS2], AF.Silu), inc="a")
                banks.release(k, "a", ta)
                b.wait("vector", "a", ta)
                if layer == 0:
                    b.wait("vector", "a", st["s_sb_a"])
                b.op("vector", lambda e, m=m, sl=sl, lo=lo: e.tensor_mul(
                    tg2[:, sl:sl + TS2], tg[:, sl:sl + TS2], A32(m, lo, TS2)))
                b.drain("vector")
                cb[(m, sub)] = b.op("vector", lambda e, m=m, sl=sl, lo=lo: e.tensor_add(
                    XB(m, lo, TS2), XB(m, lo, TS2), tg2[:, sl:sl + TS2]), inc="v")
            for q in range(4):
                s_d.done(4 * m + q, t)
            if stat:
                b.wait("scalar", "v", cb[(m, 1)])
                if not quant:
                    b.wait("scalar", "pe", st["xn_last_pe"])
                    b.op("scalar", lambda e, m=m: e.activation(XN(m), XB(m), AF.Copy))
                    ta_stat = b.op("scalar", lambda e, m=m: e.activation(
                        bufA[:, m * T:(m + 1) * T], XB(m), AF.Square),
                        inc="a" if m == HC - 1 else None)
                else:
                    b.wait("scalar", *xtok)
                    ta_stat = b.op("scalar", lambda e, m=m: e.activation(
                        bufA[:, m * T:(m + 1) * T], XN(m), AF.Square),
                        inc="a" if m == HC - 1 else None)
        st["combine_v"] = cb[(HC - 1, 1)]
        st["h_free_pe"] = t
        if stat:
            bm = banks.alloc()
            bs = banks.alloc()
            b.wait("tensor", "a", ta_stat)
            if xtok is not None:
                b.wait("tensor", *xtok)
            for c in range(HC):
                b.op("tensor", lambda e, c=c, k=bm: e.matmul(
                    pb[k][0:1, :], ones_col[:, :], XN(c),
                    start=(c == 0), stop=(c == HC - 1)),
                    inc="pe" if c == HC - 1 else None)
            for c in range(HC):
                b.op("tensor", lambda e, c=c, k=bs: e.matmul(
                    pb[k][0:1, :], ones_col[:, :], bufA[:, c * T:(c + 1) * T],
                    start=(c == 0), stop=(c == HC - 1)),
                    inc="pe" if c == HC - 1 else None)
            st["ln_pre"] = (bm, bs, b.cnt["pe"])

    def cd(streams):
        s_c, s_dw = streams["cT"], streams["dT"]
        b.wait("scalar", "v", st["combine_v"])
        b.drain("vector")
        xv = {}
        for m in range(HC):
            kc = banks.alloc()
            tc = mm_family(s_c, m, kc, lambda kk: XN(kk), HC)
            kd = banks.alloc()
            td = mm_family(s_dw, m, kd, lambda kk: XN(kk), HC)
            sl = (m % 2) * T
            b.wait("scalar", "pe", td)
            b.wait("scalar", "v", xv.get(m - 2, 0))
            ta = b.op("scalar", lambda e, k=kd, m=m, sl=sl: e.activation(
                tg[:, sl:sl + T], pb[k][:, :], AF.Sigmoid, bias=cdb[:, 16 + m:17 + m]), inc="a")
            banks.release(kd, "a", ta)
            b.wait("vector", "a", ta)
            b.wait("vector", "pe", tc)
            xv[m] = b.op("vector", lambda e, k=kc, m=m, sl=sl: e.scalar_tensor_tensor(
                A32(m), pb[k][:, :], cdb[:, m:m + 1], tg[:, sl:sl + T], ALU.add, ALU.mult), inc="v")
            banks.release(kc, "v", xv[m])
        st["xp_done_v"] = xv[HC - 1]
        st["cd_v"] = xv[HC - 1]

    def scan1():
        b.wait("vector", "a", st["prologue_a"])
        b.drain("vector")
        for c in range(HC):
            b.op("vector", lambda e, c=c: e.tensor_tensor_scan(
                A32(c), ascl[:, c:c + 1].broadcast_to([128, T]), A32(c),
                st1[:, c:c + 1], ALU.mult, ALU.add))
        b.drain("vector")
        for c in range(HC):
            b.op("vector", lambda e, c=c: e.tensor_copy(st1[:, c:c + 1], A32(c, T - 1, 1)))
        b.drain("vector")
        for c in range(HC):
            last = (c == HC - 1)
            b.op("vector", lambda e, c=c: e.tensor_scalar_mul(A32(c), A32(c), abr[:, c:c + 1]),
                 inc="v" if last else None)
        b.drain("vector")
        st["s1_v"] = b.cnt["v"]

    def store_out(tile):
        b.wait("sync", "v", st["combine_v"])
        src = xb[:, :].rearrange("p (c t) -> p c t", c=HC)
        dst = yT.rearrange("(c p) s -> p c s", p=128)[:, :, tile * T:(tile + 1) * T]
        st["out_dx"] = b.sdma(lambda e, d=dst, s=src: e.dma_start(d, s))

    prologue()

    def mk_streams(tile):
        cfg = [("wf", 16, 2048, 1, 2), ("wi", 16, 2048, 1, 2), ("wv", 16, 2048, 1, 2),
               ("wq", 16, 2048, 1, 2), ("wro", 16, 2048, 1, 1), ("wup", 16, 2048, 1, 2),
               ("wgl", 16, 2048, 1, 2), ("wdnl", 16, 2048, 1, 1),
               ("f0g", 64, 2048, 1, 2), ("f0u", 64, 2048, 1, 2), ("f0d", 64, 2048, 1, 2),
               ("cT", 16, 2048, 1, 2), ("dT", 16, 2048, 1, 2),
               ("f1g", 64, 2048, 1, 2), ("f1u", 64, 2048, 1, 2), ("f1d", 64, 2048, 1, 2)]
        out = {}
        for nm, n, ue, spu, grp in cfg:
            if nm in scr:
                if tile == 0:
                    out[nm] = PanelStream(b, ring, dr[nm], n, ue, spu, grp,
                                          scratch=scr[nm], convert=True)
                else:
                    out[nm] = PanelStream(b, ring, scr[nm], n, ue, spu, grp)
            else:
                out[nm] = PanelStream(b, ring, dr[nm], n, ue, spu, grp)
        return out

    for tile in range(n_tiles):
        streams = mk_streams(tile)
        if tile == 1:
            for i in range(4):
                b.wait("gpsimd", f"dy{i}", b.cnt[f"dy{i}"])
        load_x(tile)
        st["xn_free_pe"] = st.get("h_free_pe", 0)
        layernorm(0)
        fiv(streams)
        scan0()
        local(streams)
        q_readout(streams)
        saccum(streams)
        ffnA(0, streams)
        ffnB(0, streams, tile)
        st["xn_free_pe"] = st["h_free_pe"]
        layernorm(1)
        cd(streams)
        scan1()
        ffnA(1, streams)
        ffnB(1, streams, tile)
        store_out(tile)

    b.wait("sync", "dx", b.cnt["dx"])

    with es, nc.Block() as block:
        @block.tensor
        def _(eng):
            for fn in b.ops["tensor"]:
                fn(eng)

        @block.vector
        def _(eng):
            for fn in b.ops["vector"]:
                fn(eng)

        @block.scalar
        def _(eng):
            for fn in b.ops["scalar"]:
                fn(eng)

        @block.gpsimd
        def _(eng):
            for fn in b.ops["gpsimd"]:
                fn(eng)

        @block.sync
        def _(eng):
            for fn in b.ops["sync"]:
                fn(eng)

    return nc




def kernel(**inputs):
    w = host_prep(inputs)
    xs = np.asarray(inputs["L0_attn_x"])
    nc = build_nc()
    in_maps = []
    for bidx in range(NCORES):
        m = dict(w)
        m["xT"] = np.ascontiguousarray(xs[bidx].T)
        in_maps.append(m)
    res = run_bass_kernel_spmd(nc, in_maps, core_ids=list(range(NCORES)))
    out = np.stack([np.asarray(res.results[bidx]["yT"]).T for bidx in range(NCORES)])
    return out.astype(np.float32)


# version 10
# speedup vs baseline: 1.1133x; 1.1133x over previous
import numpy as np
from contextlib import ExitStack

import concourse.bass as bass
import concourse.mybir as mybir
from concourse.bass_utils import run_bass_kernel_spmd

F32 = mybir.dt.float32
BF16 = mybir.dt.bfloat16
U8 = mybir.dt.uint8
AF = mybir.ActivationFunctionType
ALU = mybir.AluOpType

H = 2048
HC = H // 128
IFF = 8192
IC = IFF // 128
NCORES = 8
EPS = 1e-5



def _panelize(w):
    K, M = w.shape
    p = w.reshape(K // 128, 128, M // 128, 128).transpose(2, 1, 0, 3)
    return np.ascontiguousarray(p.reshape(M // 128, 128, K))


def _vec128(v):
    n = v.shape[0] // 128
    return np.ascontiguousarray(np.asarray(v).reshape(n, 128).T)


def _low_byte(w_int32):
    a = np.ascontiguousarray(np.asarray(w_int32))
    return np.ascontiguousarray(a.view(np.uint8).reshape(*a.shape, 4)[..., 0])


_L0_ORDER = [16 * j + gh for gh in range(16) for j in range(4)]


def _halves(pan):
    n, _, K = pan.shape
    return np.ascontiguousarray(
        pan.reshape(n, 128, 4, K // 4).transpose(0, 2, 1, 3).reshape(4 * n, 128, K // 4))


def host_prep(inp):
    w = {}
    w["wf"] = _panelize(np.asarray(inp["L0_attn_w_f"]))
    w["wi"] = _panelize(np.asarray(inp["L0_attn_w_i"]))
    w["wv"] = _panelize(np.asarray(inp["L0_attn_w_v"]))
    w["wq"] = _panelize(np.asarray(inp["L0_attn_w_q"]))
    w["wro"] = _panelize(np.asarray(inp["L0_attn_w_ro"]))
    w["wup"] = _panelize(np.asarray(inp["L0_attn_w_up"]))
    w["wgl"] = _panelize(np.asarray(inp["L0_attn_w_gate"]))
    w["wdnl"] = _panelize(np.asarray(inp["L0_attn_w_down"]))

    def perm_cols(m):
        return np.ascontiguousarray(m.reshape(m.shape[0], 2048, 4).transpose(0, 2, 1).reshape(m.shape[0], 8192))

    def perm_rows(m):
        return np.ascontiguousarray(m.reshape(2048, 4, m.shape[1]).transpose(1, 0, 2).reshape(8192, m.shape[1]))

    fg = _panelize(perm_cols(np.asarray(inp["L0_ffn_w_gate"])))
    fu = _panelize(perm_cols(np.asarray(inp["L0_ffn_w_up"])))
    w["f0g"] = np.ascontiguousarray(fg[_L0_ORDER])
    w["f0u"] = np.ascontiguousarray(fu[_L0_ORDER])
    w["f0d"] = _halves(_panelize(perm_rows(np.asarray(inp["L0_ffn_w_down"]))))

    w["cT"] = _panelize(np.ascontiguousarray(np.asarray(inp["L1_attn_C_w"]).T))
    w["dT"] = _panelize(np.ascontiguousarray(np.asarray(inp["L1_attn_D_w"]).T))

    w["f1g"] = _panelize(np.ascontiguousarray(_low_byte(inp["L1_ffn_gate_w"]).T))
    w["f1u"] = _panelize(np.ascontiguousarray(_low_byte(inp["L1_ffn_up_w"]).T))
    w["f1d"] = _halves(_panelize(np.ascontiguousarray(_low_byte(inp["L1_ffn_down_w"]).T)))

    w["cdb"] = np.ascontiguousarray(
        np.concatenate([_vec128(inp["L1_attn_C_b"]), _vec128(inp["L1_attn_D_b"])], axis=1))
    w["logd"] = _vec128(inp["L1_attn_log_delta"])
    w["gsc"] = _vec128(inp["L1_ffn_gate_scale"])
    w["usc"] = _vec128(inp["L1_ffn_up_scale"])
    w["dsc"] = _vec128(inp["L1_ffn_down_scale"])
    w["inist"] = _vec128(inp["L0_attn_init_state"])
    w["amat"] = np.ascontiguousarray(
        np.asarray(inp["L1_attn_A"]).reshape(16, 128, 64).transpose(1, 0, 2).reshape(128, 1024))
    w["bmat"] = np.ascontiguousarray(
        np.asarray(inp["L1_attn_B"]).reshape(16, 128, 64).transpose(1, 0, 2).reshape(128, 1024))
    return w




class Build:
    N_WLANES = 8

    def __init__(self, nc, sems):
        self.nc = nc
        self.sems = sems
        self.ops = {e: [] for e in ("tensor", "vector", "scalar", "gpsimd", "sync")}
        self.cnt = {k: 0 for k in sems}
        self.last_wait = {e: {} for e in self.ops}
        self.wlane_ptr = 0

    def emit(self, eng, fn):
        self.ops[eng].append(fn)

    def wait(self, eng, sem, target):
        if target <= 0 or self.last_wait[eng].get(sem, 0) >= target:
            return
        self.last_wait[eng][sem] = target
        mult = 16 if (sem == "dx" or sem.startswith("dw")) else 1
        s = self.sems[sem]
        self.emit(eng, lambda e, s=s, t=target * mult: e.wait_ge(s, t))

    def op(self, eng, fn, inc=None):
        if inc is None:
            self.emit(eng, fn)
            return None
        self.cnt[inc] += 1
        amt = 16 if (inc == "dx" or inc.startswith("dw")) else 1
        s = self.sems[inc]
        self.emit(eng, lambda e, f=fn, s=s, a=amt: f(e).then_inc(s, a))
        return self.cnt[inc]

    def wdma(self, fn):
        lane = f"dw{self.wlane_ptr}"
        self.wlane_ptr = (self.wlane_ptr + 1) % self.N_WLANES
        self.wait("gpsimd", lane, self.cnt[lane])
        t = self.op("gpsimd", fn, inc=lane)
        return (lane, t)

    def sdma(self, fn):
        self.wait("sync", "dx", self.cnt["dx"])
        return self.op("sync", fn, inc="dx")

    def drain(self, eng):
        self.emit(eng, lambda e: e.drain())


class Banks:
    def __init__(self, b, tensors):
        self.b = b
        self.t = tensors
        self.state = [("a", 0)] * len(tensors)
        self.held = [False] * len(tensors)
        self.ptr = 0

    def alloc(self):
        for _ in range(len(self.t)):
            i = self.ptr
            self.ptr = (self.ptr + 1) % len(self.t)
            if not self.held[i]:
                sem, tgt = self.state[i]
                self.b.wait("tensor", sem, tgt)
                self.held[i] = True
                return i
        raise RuntimeError("psum banks exhausted")

    def release(self, i, sem, target):
        self.held[i] = False
        self.state[i] = (sem, target)


class Ring:
    def __init__(self, b, tensor, n):
        self.b = b
        self.t = tensor
        self.n = n
        self.free_at = [0] * n
        self.ptr = 0

    def alloc(self, k):
        if self.ptr + k > self.n:
            self.ptr = 0
        base = self.ptr
        self.ptr += k
        return base, max(self.free_at[base:base + k])

    def mark(self, base, k, pe_target):
        for s in range(base, base + k):
            self.free_at[s] = max(self.free_at[s], pe_target)


class PanelStream:
    def __init__(self, b, ring, dram_ap, n_units, unit_elems, slots_per_unit, group):
        self.b = b
        self.ring = ring
        self.dram = dram_ap
        self.n_units = n_units
        self.ue = unit_elems
        self.spu = slots_per_unit
        self.group = group
        self.next_fetch = 0
        self.inflight = {}

    def ensure(self, unit):
        while self.next_fetch <= unit:
            u0 = self.next_fetch
            g = min(self.group, self.n_units - u0)
            base, pe_tgt = self.ring.alloc(g * self.spu)
            self.b.wait("gpsimd", "pe", pe_tgt)
            if g == 1:
                src = self.dram[u0]
                dst = self.ring.t[:, base * 2048: base * 2048 + self.ue]
            else:
                src = self.dram[u0:u0 + g].transpose([1, 0, 2])
                dst = self.ring.t[:, base * 2048: base * 2048 + g * self.ue].rearrange(
                    "p (g n) -> p g n", g=g)
            tok = self.b.wdma(lambda e, d=dst, s=src: e.dma_start(d, s))
            for i in range(g):
                self.inflight[u0 + i] = (base + i * self.spu, tok)
            self.next_fetch += g

    def unit(self, u):
        self.ensure(u)
        return self.inflight[u]

    def lhsT(self, slot_base, k):
        return self.ring.t[:, slot_base * 2048 + k * 128: slot_base * 2048 + (k + 1) * 128]

    def done(self, u, pe_target):
        base, _ = self.inflight.pop(u)
        self.ring.mark(base, self.spu, pe_target)




def build_nc(S=2048, T=512):
    n_tiles = S // T
    TS2 = T // 2
    nc = bass.Bass("TRN2", target_bir_lowering=False)

    xT = nc.dram_tensor("xT", [H, S], F32, kind="ExternalInput").ap()
    yT = nc.dram_tensor("yT", [H, S], F32, kind="ExternalOutput").ap()
    dr = {}
    for nm, units, ue, dt in [
        ("wf", 16, 2048, F32), ("wi", 16, 2048, F32), ("wv", 16, 2048, F32),
        ("wq", 16, 2048, F32), ("wro", 16, 2048, F32), ("wup", 16, 2048, F32),
        ("wgl", 16, 2048, F32), ("wdnl", 16, 2048, F32),
        ("f0g", 64, 2048, F32), ("f0u", 64, 2048, F32), ("f0d", 64, 2048, F32),
        ("cT", 16, 2048, F32), ("dT", 16, 2048, F32),
        ("f1g", 64, 2048, U8), ("f1u", 64, 2048, U8), ("f1d", 64, 2048, U8),
    ]:
        dr[nm] = nc.dram_tensor(nm, [units, 128, ue], dt, kind="ExternalInput").ap()
    for nm, cols in [("cdb", 32), ("logd", 16), ("gsc", 64), ("usc", 64),
                     ("dsc", 16), ("inist", 16)]:
        dr[nm] = nc.dram_tensor(nm, [128, cols], F32, kind="ExternalInput").ap()
    dr["amat"] = nc.dram_tensor("amat", [128, 1024], F32, kind="ExternalInput").ap()
    dr["bmat"] = nc.dram_tensor("bmat", [128, 1024], F32, kind="ExternalInput").ap()

    es = ExitStack()
    sb = lambda nm, shape, dt: es.enter_context(nc.sbuf_tensor(nm, shape, dt))

    xb = sb("xb", [128, HC * T], F32)
    xn = sb("xn", [128, HC * T], BF16)
    bufA = sb("bufA", [128, 2 * HC * T], BF16)
    bufB = sb("bufB", [128, 2 * HC * T], BF16)
    ro = sb("ro", [128, HC * T], BF16)
    loc = sb("loc", [128, HC * T], BF16)
    ring_t = sb("ring", [128, 8 * 2048], BF16)
    h32 = sb("h32", [128, 4 * T], F32)
    sp = sb("sp", [128, 8 * TS2], F32)
    tg = sb("tg", [128, 2 * T], F32)
    tg2 = sb("tg2", [128, 2 * T], F32)
    stat_bf = sb("stat_bf", [1, 2 * T], BF16)
    ones_col = sb("ones_col", [128, 1], BF16)
    ones_row = sb("ones_row", [1, 128], BF16)
    n128_row = sb("n128_row", [1, 128], BF16)
    cdb = sb("cdb_s", [128, 32], F32)
    ascl = sb("ascl", [128, 16], F32)
    abr = sb("abr", [128, 16], F32)
    gsc = sb("gsc_s", [128, 64], F32)
    usc = sb("usc_s", [128, 64], F32)
    dsc = sb("dsc_s", [128, 16], F32)
    st0 = sb("st0", [128, 16], F32)
    st1 = sb("st1", [128, 16], F32)

    bufA32 = bufA[:, :].bitcast(F32)
    bufB32 = bufB[:, :].bitcast(F32)

    pb = [es.enter_context(nc.psum_tensor(f"pb{k}", [128, T], F32)) for k in range(8)]

    sems = {nm: es.enter_context(nc.semaphore("sem_" + nm))
            for nm in ["dx", "pe", "v", "a", "g"] + [f"dw{i}" for i in range(Build.N_WLANES)]}

    b = Build(nc, sems)
    banks = Banks(b, pb)
    ring = Ring(b, ring_t, 8)

    def A32(c, lo=0, w=T):
        return bufA32[:, c * T + lo: c * T + lo + w]

    def B32(c, lo=0, w=T):
        return bufB32[:, c * T + lo: c * T + lo + w]

    def XB(c, lo=0, w=T):
        return xb[:, c * T + lo: c * T + lo + w]

    def XN(c, lo=0, w=T):
        return xn[:, c * T + lo: c * T + lo + w]

    def h_ap(c, lo=0, w=T):
        if c < 32:
            t, off = bufB, c * T
        elif c < 48:
            t, off = ro, (c - 32) * T
        else:
            t, off = loc, (c - 48) * T
        return t[:, off + lo: off + lo + w]

    st = {}

    def prologue():
        b.op("gpsimd", lambda e: e.memset(ones_col[:, :], 1.0), inc="g")
        b.op("gpsimd", lambda e: e.memset(ones_row[:, :], 1.0), inc="g")
        b.op("gpsimd", lambda e: e.memset(n128_row[:, :], -128.0), inc="g")
        for nm, dst in [("cdb", cdb), ("gsc", gsc), ("usc", usc), ("dsc", dsc),
                        ("inist", st0), ("logd", ascl)]:
            b.sdma(lambda e, d=dst, s=dr[nm]: e.dma_start(d[:, :], s))
        b.sdma(lambda e: e.dma_start(h32[:, 0:1024], dr["amat"]))
        pro_dx = b.sdma(lambda e: e.dma_start(h32[:, 1024:2048], dr["bmat"]))
        st["pro_dx"] = pro_dx
        b.wait("vector", "dx", pro_dx)
        b.wait("scalar", "dx", pro_dx)
        b.op("vector", lambda e: e.memset(st1[:, :], 0.0))
        b.op("vector", lambda e: e.tensor_mul(sp[:, 0:1024], h32[:, 0:1024], h32[:, 1024:2048]))
        b.drain("vector")
        for c in range(16):
            b.op("vector", lambda e, c=c: e.tensor_reduce(
                abr[:, c:c + 1], sp[:, c * 64:(c + 1) * 64], mybir.AxisListType.X, ALU.add))
        b.drain("vector")
        tv = b.op("vector", lambda e: e.tensor_single_scalar(sp[:, 16:32], ascl[:, :], 0.0, ALU.max), inc="v")
        b.wait("scalar", "v", tv)
        b.op("scalar", lambda e: e.activation(sp[:, 0:16], ascl[:, :], AF.Abs))
        b.drain("scalar")
        b.op("scalar", lambda e: e.activation(sp[:, 32:48], sp[:, 0:16], AF.Exp, scale=-1.0))
        b.drain("scalar")
        ta = b.op("scalar", lambda e: e.activation(sp[:, 48:64], sp[:, 32:48], AF.Ln, bias=1.0), inc="a")
        b.wait("vector", "a", ta)
        tv = b.op("vector", lambda e: e.tensor_add(sp[:, 64:80], sp[:, 16:32], sp[:, 48:64]), inc="v")
        b.wait("scalar", "v", tv)
        st["prologue_a"] = b.op("scalar", lambda e: e.activation(ascl[:, :], sp[:, 64:80], AF.Exp, scale=-1.0), inc="a")

    def load_x(tile):
        b.wait("sync", "v", st.get("combine_v", 0))
        b.wait("sync", "dx", st.get("out_dx", 0))
        src = xT.rearrange("(c p) s -> p c s", p=128)[:, :, tile * T:(tile + 1) * T]
        dst = xb[:, :].rearrange("p (c t) -> p c t", c=HC)
        st["xb_dx"] = b.sdma(lambda e, d=dst, s=src: e.dma_start(d, s))

    def layernorm(layer):
        pre = st.pop("ln_pre", None)
        b.wait("scalar", "dx", st["xb_dx"])
        b.wait("scalar", "v", st.get("combine_v", 0))
        if pre is None:
            b.wait("scalar", "pe", st.get("xn_free_pe", 0))
            b.op("scalar", lambda e: e.activation(bufA[:, 0:HC * T], xb[:, :], AF.Copy))
            ta_sq = b.op("scalar", lambda e: e.activation(xn[:, :], xb[:, :], AF.Square), inc="a")
            bm = banks.alloc()
            bs = banks.alloc()
            b.wait("tensor", "a", ta_sq)
            b.wait("tensor", "g", 3)
            for c in range(HC):
                b.op("tensor", lambda e, c=c, k=bm: e.matmul(
                    pb[k][0:1, :], ones_col[:, :], bufA[:, c * T:(c + 1) * T],
                    start=(c == 0), stop=(c == HC - 1)),
                    inc="pe" if c == HC - 1 else None)
            for c in range(HC):
                b.op("tensor", lambda e, c=c, k=bs: e.matmul(
                    pb[k][0:1, :], ones_col[:, :], XN(c),
                    start=(c == 0), stop=(c == HC - 1)),
                    inc="pe" if c == HC - 1 else None)
            t_ps = b.cnt["pe"]
        else:
            bm, bs, t_ps = pre
        SL = lambda i: sp[0:1, i * T:(i + 1) * T]
        b.wait("vector", "pe", t_ps)
        b.op("vector", lambda e: e.tensor_single_scalar(SL(0), pb[bm][0:1, :], 1.0 / H, ALU.mult))
        tv = b.op("vector", lambda e: e.tensor_single_scalar(SL(1), pb[bs][0:1, :], 1.0 / H, ALU.mult), inc="v")
        banks.release(bm, "v", tv)
        banks.release(bs, "v", tv)
        b.wait("scalar", "v", tv)
        ta = b.op("scalar", lambda e: e.activation(SL(2), SL(0), AF.Square), inc="a")
        b.wait("vector", "a", ta)
        b.drain("vector")
        b.op("vector", lambda e: e.tensor_sub(SL(1), SL(1), SL(2)))
        b.drain("vector")
        tv = b.op("vector", lambda e: e.tensor_single_scalar(SL(1), SL(1), EPS, ALU.add), inc="v")
        b.wait("scalar", "v", tv)
        ta = b.op("scalar", lambda e: e.activation(SL(2), SL(1), AF.Sqrt), inc="a")
        b.wait("vector", "a", ta)
        b.drain("vector")
        b.op("vector", lambda e: e.reciprocal(SL(1), SL(2)))
        b.drain("vector")
        tv = b.op("vector", lambda e: e.scalar_tensor_tensor(SL(3), SL(0), -1.0, SL(1), ALU.mult, ALU.mult), inc="v")
        b.wait("scalar", "v", tv)
        b.op("scalar", lambda e: e.activation(stat_bf[0:1, 0:T], SL(1), AF.Copy))
        ta = b.op("scalar", lambda e: e.activation(stat_bf[0:1, T:2 * T], SL(3), AF.Copy), inc="a")
        ba = banks.alloc()
        bb = banks.alloc()
        b.wait("tensor", "a", ta)
        b.op("tensor", lambda e, k=ba: e.matmul(pb[k][:, :], ones_row[:, :], stat_bf[0:1, 0:T],
                                                start=True, stop=True))
        t_bc = b.op("tensor", lambda e, k=bb: e.matmul(pb[k][:, :], ones_row[:, :], stat_bf[0:1, T:2 * T],
                                                       start=True, stop=True), inc="pe")
        b.wait("vector", "pe", t_bc)
        b.wait("vector", "dx", st["xb_dx"])
        b.wait("vector", "v", st.get("combine_v", 0))
        for c in range(HC):
            b.op("vector", lambda e, c=c, k=ba: e.tensor_mul(B32(c), XB(c), pb[k][:, :]))
        b.drain("vector")
        for c in range(HC):
            last = (c == HC - 1)
            b.op("vector", lambda e, c=c, k=bb: e.tensor_add(XN(c), B32(c), pb[k][:, :]),
                 inc="v" if last else None)
        b.drain("vector")
        st["xn_v"] = b.cnt["v"]
        banks.release(ba, "v", st["xn_v"])
        banks.release(bb, "v", st["xn_v"])

    def mm_family(stream, u, bank, rhs_fn, n_k, w=T, extra=None, consume=True):
        base, dw_tok = stream.unit(u)
        b.wait("tensor", *dw_tok)
        b.wait("tensor", "v", st.get("xn_v", 0))
        out = pb[bank][:, 0:w]
        for k in range(n_k):
            last = (k == n_k - 1 and extra is None)
            b.op("tensor", lambda e, k=k, out=out, base=base, last=last: e.matmul(
                out, stream.lhsT(base, k), rhs_fn(k), start=(k == 0), stop=last),
                inc="pe" if last else None)
        if extra is not None:
            b.op("tensor", lambda e, out=out: e.matmul(out, extra[0], extra[1],
                                                       start=False, stop=True), inc="pe")
        t = b.cnt["pe"]
        if consume:
            stream.done(u, t)
        return t

    def fiv(streams):
        s_f, s_i, s_v = streams["wf"], streams["wi"], streams["wv"]
        for m in range(HC):
            k = banks.alloc()
            t = mm_family(s_f, m, k, lambda kk: XN(kk), HC)
            b.wait("scalar", "pe", t)
            ta = b.op("scalar", lambda e, k=k, m=m: e.activation(A32(m), pb[k][:, :], AF.Tanh), inc="a")
            banks.release(k, "a", ta)
        st["f_a"] = ta
        gp = {}
        for m in range(HC):
            ki = banks.alloc()
            ti = mm_family(s_i, m, ki, lambda kk: XN(kk), HC)
            kv = banks.alloc()
            tvm = mm_family(s_v, m, kv, lambda kk: XN(kk), HC)
            sl = (m % 2) * T
            b.wait("scalar", "v", gp.get(m - 2, 0))
            b.wait("scalar", "pe", ti)
            b.op("scalar", lambda e, k=ki, sl=sl: e.activation(tg[:, sl:sl + T], pb[k][:, :], AF.Sigmoid))
            b.wait("scalar", "pe", tvm)
            ta = b.op("scalar", lambda e, k=kv, sl=sl: e.activation(tg2[:, sl:sl + T], pb[k][:, :], AF.Silu), inc="a")
            banks.release(ki, "a", ta)
            banks.release(kv, "a", ta)
            b.wait("vector", "a", ta)
            gp[m] = b.op("vector", lambda e, m=m, sl=sl: e.tensor_mul(B32(m), tg[:, sl:sl + T], tg2[:, sl:sl + T]), inc="v")
        st["g_v"] = gp[HC - 1]
        st["fiv_a"] = ta

    def scan0():
        b.wait("vector", "a", st["f_a"])
        b.wait("vector", "dx", st["pro_dx"])
        b.drain("vector")
        for c in range(HC):
            b.op("vector", lambda e, c=c: e.tensor_tensor_scan(
                B32(c), A32(c), B32(c), st0[:, c:c + 1], ALU.mult, ALU.add))
        b.drain("vector")
        for c in range(HC):
            last = (c == HC - 1)
            b.op("vector", lambda e, c=c: e.tensor_copy(st0[:, c:c + 1], B32(c, T - 1, 1)),
                 inc="v" if last else None)
        st["scan_v"] = b.cnt["v"]
        b.drain("vector")

    def local(streams):
        s_u, s_g = streams["wup"], streams["wgl"]
        b.wait("scalar", "v", st["g_v"])
        lv = {}
        for m in range(HC):
            ku = banks.alloc()
            tu = mm_family(s_u, m, ku, lambda kk: XN(kk), HC)
            kg = banks.alloc()
            tgm = mm_family(s_g, m, kg, lambda kk: XN(kk), HC)
            sl = (m % 2) * T
            b.wait("scalar", "pe", tgm)
            b.wait("scalar", "v", lv.get(m - 2, 0))
            ta = b.op("scalar", lambda e, k=kg, sl=sl: e.activation(tg2[:, sl:sl + T], pb[k][:, :], AF.Silu), inc="a")
            banks.release(kg, "a", ta)
            b.wait("vector", "a", ta)
            b.wait("vector", "pe", tu)
            lv[m] = b.op("vector", lambda e, k=ku, m=m, sl=sl: e.tensor_mul(
                loc[:, m * T:(m + 1) * T], pb[k][:, :], tg2[:, sl:sl + T]), inc="v")
            banks.release(ku, "v", lv[m])
        st["loc_done_v"] = lv[HC - 1]

    def q_readout(streams):
        s_q = streams["wq"]
        b.wait("vector", "a", st["fiv_a"])
        ra = {}
        for m in range(HC):
            k = banks.alloc()
            t = mm_family(s_q, m, k, lambda kk: XN(kk), HC)
            sl = (m % 2) * T
            b.wait("vector", "pe", t)
            b.wait("vector", "v", st["scan_v"])
            b.wait("vector", "a", ra.get(m - 2, 0))
            tvp = b.op("vector", lambda e, k=k, m=m, sl=sl: e.tensor_mul(tg[:, sl:sl + T], pb[k][:, :], B32(m)), inc="v")
            banks.release(k, "v", tvp)
            b.wait("scalar", "v", tvp)
            ra[m] = b.op("scalar", lambda e, m=m, sl=sl: e.activation(
                ro[:, m * T:(m + 1) * T], tg[:, sl:sl + T], AF.Silu), inc="a")
        st["ro_done_a"] = ra[HC - 1]
        st["q_v"] = b.cnt["v"]

    def saccum(streams):
        s_ro, s_dn = streams["wro"], streams["wdnl"]
        for m in range(HC):
            k = banks.alloc()
            base_r, dw_r = s_ro.unit(m)
            base_d, dw_d = s_dn.unit(m)
            b.wait("tensor", *dw_r)
            b.wait("tensor", *dw_d)
            b.wait("tensor", "a", st["ro_done_a"])
            b.wait("tensor", "v", st["loc_done_v"])
            for kk in range(HC):
                b.op("tensor", lambda e, kk=kk, base=base_r, k=k: e.matmul(
                    pb[k][:, :], s_ro.lhsT(base, kk), ro[:, kk * T:(kk + 1) * T],
                    start=(kk == 0), stop=False))
            for kk in range(HC):
                b.op("tensor", lambda e, kk=kk, base=base_d, k=k: e.matmul(
                    pb[k][:, :], s_dn.lhsT(base, kk), loc[:, kk * T:(kk + 1) * T],
                    start=False, stop=(kk == HC - 1)),
                    inc="pe" if kk == HC - 1 else None)
            t = b.cnt["pe"]
            s_ro.done(m, t)
            s_dn.done(m, t)
            b.wait("scalar", "pe", t)
            b.wait("scalar", "v", st["scan_v"])
            ta = b.op("scalar", lambda e, k=k, m=m: e.activation(A32(m), pb[k][:, :], AF.Copy), inc="a")
            banks.release(k, "a", ta)
        st["s_sb_a"] = ta
        st["h_free_pe"] = t

    def zrow(slot):
        k = banks.alloc()
        b.wait("tensor", "v", st["xn_v"])
        for c in range(HC):
            b.op("tensor", lambda e, c=c, k=k: e.matmul(
                pb[k][0:1, :], ones_col[:, :], XN(c), start=(c == 0), stop=(c == HC - 1)),
                inc="pe" if c == HC - 1 else None)
        t = b.cnt["pe"]
        b.wait("scalar", "pe", t)
        ta = b.op("scalar", lambda e, k=k, slot=slot: e.activation(
            stat_bf[0:1, slot * T:(slot + 1) * T], pb[k][0:1, :], AF.Copy), inc="a")
        banks.release(k, "a", ta)
        return ta

    def sparsify(gh, t_h, inc_last=False):
        A_ = lambda j: sp[:, j * TS2:(j + 1) * TS2]
        T_ = lambda k: sp[:, (4 + k) * TS2:(5 + k) * TS2]
        H_ = lambda j, s: h32[:, j * T + s * TS2: j * T + s * TS2 + TS2]
        out_t = None
        for s in (0, 1):
            b.drain("vector")
            for j in range(4):
                b.op("vector", lambda e, j=j, s=s: e.tensor_single_scalar(
                    A_(j), H_(j, s), -1.0, ALU.mult))
            b.drain("vector")
            for j in range(4):
                b.op("vector", lambda e, j=j, s=s: e.tensor_tensor(
                    A_(j), A_(j), H_(j, s), ALU.max))
            b.drain("vector")
            b.op("vector", lambda e: e.tensor_max(T_(0), A_(0), A_(1)))
            b.op("vector", lambda e: e.tensor_tensor(T_(1), A_(0), A_(1), ALU.min))
            b.op("vector", lambda e: e.tensor_max(T_(2), A_(2), A_(3)))
            b.op("vector", lambda e: e.tensor_tensor(T_(3), A_(2), A_(3), ALU.min))
            b.drain("vector")
            b.op("vector", lambda e: e.tensor_tensor(T_(0), T_(0), T_(2), ALU.min))
            b.op("vector", lambda e: e.tensor_max(T_(1), T_(1), T_(3)))
            b.drain("vector")
            b.op("vector", lambda e: e.tensor_max(T_(0), T_(0), T_(1)))
            b.drain("vector")
            for j in range(4):
                b.op("vector", lambda e, j=j: e.tensor_tensor(A_(j), A_(j), T_(0), ALU.is_ge))
            b.drain("vector")
            for j in range(4):
                c = 16 * j + gh
                last = (s == 1 and j == 3)
                out_t = b.op("vector", lambda e, j=j, s=s, c=c: e.tensor_mul(
                    h_ap(c, s * TS2, TS2), A_(j), H_(j, s)), inc="v" if last else None)
        st["sp_free_v"] = out_t
        b.drain("vector")
        return out_t

    def ffnA(layer, streams):
        quant = (layer == 1)
        s_g = streams["f1g" if quant else "f0g"]
        s_u = streams["f1u" if quant else "f0u"]
        if quant:
            b.wait("scalar", "v", st["cd_v"])
        else:
            b.wait("scalar", "v", st["q_v"])
        order = list(range(IC)) if quant else _L0_ORDER
        b.wait("vector", "pe", st.get("h_free_pe", 0))
        b.drain("vector")
        hp = {}
        tvh = None
        for idx, m in enumerate(order):
            kg = banks.alloc()
            tg_t = mm_family(s_g, idx, kg, lambda kk: XN(kk), HC)
            ku = banks.alloc()
            tu_t = mm_family(s_u, idx, ku, lambda kk: XN(kk), HC)
            sl = (idx % 2) * T
            b.wait("scalar", "pe", tg_t)
            b.wait("scalar", "v", hp.get(idx - 2, 0))
            if quant:
                ta = b.op("scalar", lambda e, k=kg, m=m, sl=sl: e.activation(
                    tg[:, sl:sl + T], pb[k][:, :], AF.Silu, scale=gsc[:, m:m + 1]), inc="a")
            else:
                ta = b.op("scalar", lambda e, k=kg, sl=sl: e.activation(
                    tg[:, sl:sl + T], pb[k][:, :], AF.Silu), inc="a")
            banks.release(kg, "a", ta)
            b.wait("vector", "a", ta)
            b.wait("vector", "pe", tu_t)
            if quant:
                hp[idx] = b.op("vector", lambda e, k=ku, m=m, sl=sl: e.scalar_tensor_tensor(
                    h_ap(m), pb[k][:, :], usc[:, m:m + 1], tg[:, sl:sl + T],
                    ALU.mult, ALU.mult), inc="v")
            else:
                j = m // 16
                hp[idx] = b.op("vector", lambda e, k=ku, j=j, sl=sl: e.tensor_mul(
                    h32[:, j * T:(j + 1) * T], pb[k][:, :], tg[:, sl:sl + T]), inc="v")
            banks.release(ku, "v", hp[idx])
            if not quant and m >= 48:
                tvh = sparsify(m - 48, hp[idx], inc_last=(m == 63))
        st["h_v"] = tvh if not quant else hp[IC - 1]
        st["xn_last_pe"] = tu_t

    def ffnB(layer, streams, tile=0):
        quant = (layer == 1)
        stat = (layer == 0) or (tile + 1 < n_tiles)
        xtok = None
        if quant and stat:
            b.wait("gpsimd", "pe", st["xn_last_pe"])
            srcx = xT.rearrange("(c p) s -> p c s", p=128)[:, :, (tile + 1) * T:(tile + 2) * T]
            dstx = xn[:, :].rearrange("p (c t) -> p c t", c=HC)
            xtok = b.wdma(lambda e, d=dstx, s=srcx: e.dma_start(d, s))
        s_d = streams["f1d" if quant else "f0d"]
        zh_a = 0
        if quant:
            k = banks.alloc()
            b.wait("tensor", "v", st["h_v"])
            for c in range(IC):
                b.op("tensor", lambda e, c=c, k=k: e.matmul(
                    pb[k][0:1, :], ones_col[:, :], h_ap(c), start=(c == 0), stop=(c == IC - 1)),
                    inc="pe" if c == IC - 1 else None)
            t = b.cnt["pe"]
            b.wait("scalar", "pe", t)
            zh_a = b.op("scalar", lambda e, k=k: e.activation(
                stat_bf[0:1, T:2 * T], pb[k][0:1, :], AF.Copy), inc="a")
            banks.release(k, "a", zh_a)
        b.wait("scalar", "v", st["h_v"])
        cb = {}
        for m in range(HC):
            qu = [s_d.unit(4 * m + q) for q in range(4)]
            k = banks.alloc()
            for _, tok in qu:
                b.wait("tensor", *tok)
            b.wait("tensor", "v", st["h_v"])
            if quant:
                b.wait("tensor", "a", zh_a)
            for kk in range(IC):
                base = qu[kk // 16][0]
                ko = kk % 16
                last = (kk == IC - 1 and not quant)
                b.op("tensor", lambda e, kk=kk, base=base, ko=ko, k=k, last=last: e.matmul(
                    pb[k][:, :],
                    ring_t[:, base * 2048 + ko * 128: base * 2048 + (ko + 1) * 128],
                    h_ap(kk), start=(kk == 0), stop=last),
                    inc="pe" if last else None)
            if quant:
                b.op("tensor", lambda e, k=k: e.matmul(
                    pb[k][:, :], n128_row[:, :], stat_bf[0:1, T:2 * T],
                    start=False, stop=True), inc="pe")
            t = b.cnt["pe"]
            sl = (m % 2) * T
            b.wait("scalar", "pe", t)
            b.wait("scalar", "v", cb.get(m - 2, 0))
            if quant:
                ta = b.op("scalar", lambda e, k=k, m=m, sl=sl: e.activation(
                    tg[:, sl:sl + T], pb[k][:, :], AF.Silu, scale=dsc[:, m:m + 1]), inc="a")
            else:
                ta = b.op("scalar", lambda e, k=k, sl=sl: e.activation(
                    tg[:, sl:sl + T], pb[k][:, :], AF.Silu), inc="a")
            banks.release(k, "a", ta)
            b.wait("vector", "a", ta)
            if layer == 0:
                b.wait("vector", "a", st["s_sb_a"])
            b.op("vector", lambda e, m=m, sl=sl: e.tensor_mul(
                tg2[:, sl:sl + T], tg[:, sl:sl + T], A32(m)))
            b.drain("vector")
            cb[m] = b.op("vector", lambda e, m=m, sl=sl: e.tensor_add(
                XB(m), XB(m), tg2[:, sl:sl + T]), inc="v")
            for q in range(4):
                s_d.done(4 * m + q, t)
            if stat:
                b.wait("scalar", "v", cb[m])
                if not quant:
                    b.wait("scalar", "pe", st["xn_last_pe"])
                    b.op("scalar", lambda e, m=m: e.activation(XN(m), XB(m), AF.Copy))
                    ta_stat = b.op("scalar", lambda e, m=m: e.activation(
                        bufA[:, m * T:(m + 1) * T], XB(m), AF.Square),
                        inc="a" if m == HC - 1 else None)
                else:
                    b.wait("scalar", *xtok)
                    ta_stat = b.op("scalar", lambda e, m=m: e.activation(
                        bufA[:, m * T:(m + 1) * T], XN(m), AF.Square),
                        inc="a" if m == HC - 1 else None)
        st["combine_v"] = cb[HC - 1]
        st["h_free_pe"] = t
        if stat:
            bm = banks.alloc()
            bs = banks.alloc()
            b.wait("tensor", "a", ta_stat)
            if xtok is not None:
                b.wait("tensor", *xtok)
            for c in range(HC):
                b.op("tensor", lambda e, c=c, k=bm: e.matmul(
                    pb[k][0:1, :], ones_col[:, :], XN(c),
                    start=(c == 0), stop=(c == HC - 1)),
                    inc="pe" if c == HC - 1 else None)
            for c in range(HC):
                b.op("tensor", lambda e, c=c, k=bs: e.matmul(
                    pb[k][0:1, :], ones_col[:, :], bufA[:, c * T:(c + 1) * T],
                    start=(c == 0), stop=(c == HC - 1)),
                    inc="pe" if c == HC - 1 else None)
            st["ln_pre"] = (bm, bs, b.cnt["pe"])

    def cd(streams):
        s_c, s_dw = streams["cT"], streams["dT"]
        b.wait("scalar", "v", st["combine_v"])
        b.drain("vector")
        xv = {}
        for m in range(HC):
            kc = banks.alloc()
            tc = mm_family(s_c, m, kc, lambda kk: XN(kk), HC)
            kd = banks.alloc()
            td = mm_family(s_dw, m, kd, lambda kk: XN(kk), HC)
            sl = (m % 2) * T
            b.wait("scalar", "pe", td)
            b.wait("scalar", "v", xv.get(m - 2, 0))
            ta = b.op("scalar", lambda e, k=kd, m=m, sl=sl: e.activation(
                tg[:, sl:sl + T], pb[k][:, :], AF.Sigmoid, bias=cdb[:, 16 + m:17 + m]), inc="a")
            banks.release(kd, "a", ta)
            b.wait("vector", "a", ta)
            b.wait("vector", "pe", tc)
            xv[m] = b.op("vector", lambda e, k=kc, m=m, sl=sl: e.scalar_tensor_tensor(
                A32(m), pb[k][:, :], cdb[:, m:m + 1], tg[:, sl:sl + T], ALU.add, ALU.mult), inc="v")
            banks.release(kc, "v", xv[m])
        st["xp_done_v"] = xv[HC - 1]
        st["cd_v"] = xv[HC - 1]

    def scan1():
        b.wait("vector", "a", st["prologue_a"])
        b.drain("vector")
        for c in range(HC):
            b.op("vector", lambda e, c=c: e.tensor_tensor_scan(
                A32(c), ascl[:, c:c + 1].broadcast_to([128, T]), A32(c),
                st1[:, c:c + 1], ALU.mult, ALU.add))
        b.drain("vector")
        for c in range(HC):
            b.op("vector", lambda e, c=c: e.tensor_copy(st1[:, c:c + 1], A32(c, T - 1, 1)))
        b.drain("vector")
        for c in range(HC):
            last = (c == HC - 1)
            b.op("vector", lambda e, c=c: e.tensor_scalar_mul(A32(c), A32(c), abr[:, c:c + 1]),
                 inc="v" if last else None)
        b.drain("vector")
        st["s1_v"] = b.cnt["v"]

    def store_out(tile):
        b.wait("sync", "v", st["combine_v"])
        src = xb[:, :].rearrange("p (c t) -> p c t", c=HC)
        dst = yT.rearrange("(c p) s -> p c s", p=128)[:, :, tile * T:(tile + 1) * T]
        st["out_dx"] = b.sdma(lambda e, d=dst, s=src: e.dma_start(d, s))

    prologue()

    def mk_streams():
        cfg = [("wf", 16, 2048, 1, 2), ("wi", 16, 2048, 1, 2), ("wv", 16, 2048, 1, 2),
               ("wq", 16, 2048, 1, 2), ("wro", 16, 2048, 1, 1), ("wup", 16, 2048, 1, 2),
               ("wgl", 16, 2048, 1, 2), ("wdnl", 16, 2048, 1, 1),
               ("f0g", 64, 2048, 1, 2), ("f0u", 64, 2048, 1, 2), ("f0d", 64, 2048, 1, 2),
               ("cT", 16, 2048, 1, 2), ("dT", 16, 2048, 1, 2),
               ("f1g", 64, 2048, 1, 2), ("f1u", 64, 2048, 1, 2), ("f1d", 64, 2048, 1, 2)]
        return {nm: PanelStream(b, ring, dr[nm], n, ue, spu, grp)
                for nm, n, ue, spu, grp in cfg}

    for tile in range(n_tiles):
        streams = mk_streams()
        load_x(tile)
        st["xn_free_pe"] = st.get("h_free_pe", 0)
        layernorm(0)
        fiv(streams)
        scan0()
        local(streams)
        q_readout(streams)
        saccum(streams)
        ffnA(0, streams)
        ffnB(0, streams, tile)
        st["xn_free_pe"] = st["h_free_pe"]
        layernorm(1)
        cd(streams)
        scan1()
        ffnA(1, streams)
        ffnB(1, streams, tile)
        store_out(tile)

    b.wait("sync", "dx", b.cnt["dx"])

    with es, nc.Block() as block:
        @block.tensor
        def _(eng):
            for fn in b.ops["tensor"]:
                fn(eng)

        @block.vector
        def _(eng):
            for fn in b.ops["vector"]:
                fn(eng)

        @block.scalar
        def _(eng):
            for fn in b.ops["scalar"]:
                fn(eng)

        @block.gpsimd
        def _(eng):
            for fn in b.ops["gpsimd"]:
                fn(eng)

        @block.sync
        def _(eng):
            for fn in b.ops["sync"]:
                fn(eng)

    return nc




def kernel(**inputs):
    w = host_prep(inputs)
    xs = np.asarray(inputs["L0_attn_x"])
    nc = build_nc()
    in_maps = []
    for bidx in range(NCORES):
        m = dict(w)
        m["xT"] = np.ascontiguousarray(xs[bidx].T)
        in_maps.append(m)
    res = run_bass_kernel_spmd(nc, in_maps, core_ids=list(range(NCORES)))
    out = np.stack([np.asarray(res.results[bidx]["yT"]).T for bidx in range(NCORES)])
    return out.astype(np.float32)


# version 11
# speedup vs baseline: 1.1913x; 1.0701x over previous
import numpy as np
from contextlib import ExitStack

import concourse.bass as bass
import concourse.mybir as mybir
from concourse.bass_utils import run_bass_kernel_spmd

F32 = mybir.dt.float32
BF16 = mybir.dt.bfloat16
U8 = mybir.dt.uint8
AF = mybir.ActivationFunctionType
ALU = mybir.AluOpType

H = 2048
HC = H // 128
IFF = 8192
IC = IFF // 128
NCORES = 8
EPS = 1e-5



def _panelize(w):
    K, M = w.shape
    p = w.reshape(K // 128, 128, M // 128, 128).transpose(2, 1, 0, 3)
    return np.ascontiguousarray(p.reshape(M // 128, 128, K))


def _vec128(v):
    n = v.shape[0] // 128
    return np.ascontiguousarray(np.asarray(v).reshape(n, 128).T)


def _low_byte(w_int32):
    a = np.ascontiguousarray(np.asarray(w_int32))
    return np.ascontiguousarray(a.view(np.uint8).reshape(*a.shape, 4)[..., 0])


_L0_ORDER = [16 * j + gh for gh in range(16) for j in range(4)]


def _halves(pan):
    n, _, K = pan.shape
    return np.ascontiguousarray(
        pan.reshape(n, 128, 4, K // 4).transpose(0, 2, 1, 3).reshape(4 * n, 128, K // 4))


def host_prep(inp):
    w = {}
    w["wf"] = _panelize(np.asarray(inp["L0_attn_w_f"]))
    w["wi"] = _panelize(np.asarray(inp["L0_attn_w_i"]))
    w["wv"] = _panelize(np.asarray(inp["L0_attn_w_v"]))
    w["wq"] = _panelize(np.asarray(inp["L0_attn_w_q"]))
    w["wro"] = _panelize(np.asarray(inp["L0_attn_w_ro"]))
    w["wup"] = _panelize(np.asarray(inp["L0_attn_w_up"]))
    w["wgl"] = _panelize(np.asarray(inp["L0_attn_w_gate"]))
    w["wdnl"] = _panelize(np.asarray(inp["L0_attn_w_down"]))

    def perm_cols(m):
        return np.ascontiguousarray(m.reshape(m.shape[0], 2048, 4).transpose(0, 2, 1).reshape(m.shape[0], 8192))

    def perm_rows(m):
        return np.ascontiguousarray(m.reshape(2048, 4, m.shape[1]).transpose(1, 0, 2).reshape(8192, m.shape[1]))

    fg = _panelize(perm_cols(np.asarray(inp["L0_ffn_w_gate"])))
    fu = _panelize(perm_cols(np.asarray(inp["L0_ffn_w_up"])))
    w["f0g"] = np.ascontiguousarray(fg[_L0_ORDER])
    w["f0u"] = np.ascontiguousarray(fu[_L0_ORDER])
    w["f0d"] = _halves(_panelize(perm_rows(np.asarray(inp["L0_ffn_w_down"]))))

    w["cT"] = _panelize(np.ascontiguousarray(np.asarray(inp["L1_attn_C_w"]).T))
    w["dT"] = _panelize(np.ascontiguousarray(np.asarray(inp["L1_attn_D_w"]).T))

    w["f1g"] = _panelize(np.ascontiguousarray(_low_byte(inp["L1_ffn_gate_w"]).T))
    w["f1u"] = _panelize(np.ascontiguousarray(_low_byte(inp["L1_ffn_up_w"]).T))
    w["f1d"] = _halves(_panelize(np.ascontiguousarray(_low_byte(inp["L1_ffn_down_w"]).T)))

    w["cdb"] = np.ascontiguousarray(
        np.concatenate([_vec128(inp["L1_attn_C_b"]), _vec128(inp["L1_attn_D_b"])], axis=1))
    w["logd"] = _vec128(inp["L1_attn_log_delta"])
    w["gsc"] = _vec128(inp["L1_ffn_gate_scale"])
    w["usc"] = _vec128(inp["L1_ffn_up_scale"])
    w["dsc"] = _vec128(inp["L1_ffn_down_scale"])
    w["inist"] = _vec128(inp["L0_attn_init_state"])
    w["amat"] = np.ascontiguousarray(
        np.asarray(inp["L1_attn_A"]).reshape(16, 128, 64).transpose(1, 0, 2).reshape(128, 1024))
    w["bmat"] = np.ascontiguousarray(
        np.asarray(inp["L1_attn_B"]).reshape(16, 128, 64).transpose(1, 0, 2).reshape(128, 1024))
    return w




class Build:
    N_WLANES = 8

    def __init__(self, nc, sems):
        self.nc = nc
        self.sems = sems
        self.ops = {e: [] for e in ("tensor", "vector", "scalar", "gpsimd", "sync")}
        self.cnt = {k: 0 for k in sems}
        self.last_wait = {e: {} for e in self.ops}
        self.wlane_ptr = 0

    def emit(self, eng, fn):
        self.ops[eng].append(fn)

    def wait(self, eng, sem, target):
        if target <= 0 or self.last_wait[eng].get(sem, 0) >= target:
            return
        self.last_wait[eng][sem] = target
        mult = 16 if (sem == "dx" or sem.startswith("dw")) else 1
        s = self.sems[sem]
        self.emit(eng, lambda e, s=s, t=target * mult: e.wait_ge(s, t))

    def op(self, eng, fn, inc=None):
        if inc is None:
            self.emit(eng, fn)
            return None
        self.cnt[inc] += 1
        amt = 16 if (inc == "dx" or inc.startswith("dw")) else 1
        s = self.sems[inc]
        self.emit(eng, lambda e, f=fn, s=s, a=amt: f(e).then_inc(s, a))
        return self.cnt[inc]

    def wdma(self, fn):
        lane = f"dw{self.wlane_ptr}"
        self.wlane_ptr = (self.wlane_ptr + 1) % self.N_WLANES
        self.wait("gpsimd", lane, self.cnt[lane])
        t = self.op("gpsimd", fn, inc=lane)
        return (lane, t)

    def sdma(self, fn):
        self.wait("sync", "dx", self.cnt["dx"])
        return self.op("sync", fn, inc="dx")

    def drain(self, eng):
        self.emit(eng, lambda e: e.drain())


class Banks:
    def __init__(self, b, tensors):
        self.b = b
        self.t = tensors
        self.state = [("a", 0)] * len(tensors)
        self.held = [False] * len(tensors)
        self.ptr = 0

    def alloc(self):
        for _ in range(len(self.t)):
            i = self.ptr
            self.ptr = (self.ptr + 1) % len(self.t)
            if not self.held[i]:
                sem, tgt = self.state[i]
                self.b.wait("tensor", sem, tgt)
                self.held[i] = True
                return i
        raise RuntimeError("psum banks exhausted")

    def release(self, i, sem, target):
        self.held[i] = False
        self.state[i] = (sem, target)


class Ring:
    def __init__(self, b, tensor, n):
        self.b = b
        self.t = tensor
        self.n = n
        self.free_at = [0] * n
        self.ptr = 0

    def alloc(self, k):
        if self.ptr + k > self.n:
            self.ptr = 0
        base = self.ptr
        self.ptr += k
        return base, max(self.free_at[base:base + k])

    def mark(self, base, k, pe_target):
        for s in range(base, base + k):
            self.free_at[s] = max(self.free_at[s], pe_target)


class PanelStream:
    def __init__(self, b, ring, dram_ap, n_units, unit_elems, slots_per_unit, group):
        self.b = b
        self.ring = ring
        self.dram = dram_ap
        self.n_units = n_units
        self.ue = unit_elems
        self.spu = slots_per_unit
        self.group = group
        self.next_fetch = 0
        self.inflight = {}

    def ensure(self, unit):
        while self.next_fetch <= unit:
            u0 = self.next_fetch
            g = min(self.group, self.n_units - u0)
            base, pe_tgt = self.ring.alloc(g * self.spu)
            self.b.wait("gpsimd", "pe", pe_tgt)
            if g == 1:
                src = self.dram[u0]
                dst = self.ring.t[:, base * 2048: base * 2048 + self.ue]
            else:
                src = self.dram[u0:u0 + g].transpose([1, 0, 2])
                dst = self.ring.t[:, base * 2048: base * 2048 + g * self.ue].rearrange(
                    "p (g n) -> p g n", g=g)
            tok = self.b.wdma(lambda e, d=dst, s=src: e.dma_start(d, s))
            for i in range(g):
                self.inflight[u0 + i] = (base + i * self.spu, tok)
            self.next_fetch += g

    def unit(self, u):
        self.ensure(u)
        return self.inflight[u]

    def lhsT(self, slot_base, k):
        return self.ring.t[:, slot_base * 2048 + k * 128: slot_base * 2048 + (k + 1) * 128]

    def done(self, u, pe_target):
        base, _ = self.inflight.pop(u)
        self.ring.mark(base, self.spu, pe_target)




def build_nc(S=2048, T=512):
    n_tiles = S // T
    TS2 = T // 2
    nc = bass.Bass("TRN2", target_bir_lowering=False)

    xT = nc.dram_tensor("xT", [H, S], F32, kind="ExternalInput").ap()
    yT = nc.dram_tensor("yT", [H, S], F32, kind="ExternalOutput").ap()
    dr = {}
    for nm, units, ue, dt in [
        ("wf", 16, 2048, F32), ("wi", 16, 2048, F32), ("wv", 16, 2048, F32),
        ("wq", 16, 2048, F32), ("wro", 16, 2048, F32), ("wup", 16, 2048, F32),
        ("wgl", 16, 2048, F32), ("wdnl", 16, 2048, F32),
        ("f0g", 64, 2048, F32), ("f0u", 64, 2048, F32), ("f0d", 64, 2048, F32),
        ("cT", 16, 2048, F32), ("dT", 16, 2048, F32),
        ("f1g", 64, 2048, U8), ("f1u", 64, 2048, U8), ("f1d", 64, 2048, U8),
    ]:
        dr[nm] = nc.dram_tensor(nm, [units, 128, ue], dt, kind="ExternalInput").ap()
    for nm, cols in [("cdb", 32), ("logd", 16), ("gsc", 64), ("usc", 64),
                     ("dsc", 16), ("inist", 16)]:
        dr[nm] = nc.dram_tensor(nm, [128, cols], F32, kind="ExternalInput").ap()
    dr["amat"] = nc.dram_tensor("amat", [128, 1024], F32, kind="ExternalInput").ap()
    dr["bmat"] = nc.dram_tensor("bmat", [128, 1024], F32, kind="ExternalInput").ap()

    es = ExitStack()
    sb = lambda nm, shape, dt: es.enter_context(nc.sbuf_tensor(nm, shape, dt))

    xb = sb("xb", [128, HC * T], F32)
    xn = sb("xn", [128, HC * T], BF16)
    bufA = sb("bufA", [128, 2 * HC * T], BF16)
    bufB = sb("bufB", [128, 2 * HC * T], BF16)
    ro = sb("ro", [128, HC * T], BF16)
    loc = sb("loc", [128, HC * T], BF16)
    ring_t = sb("ring", [128, 8 * 2048], BF16)
    h32 = sb("h32", [128, 4 * T], F32)
    sp = sb("sp", [128, 8 * TS2], F32)
    tg = sb("tg", [128, 2 * T], F32)
    tg2 = sb("tg2", [128, 2 * T], F32)
    stat_bf = sb("stat_bf", [1, 2 * T], BF16)
    ones_col = sb("ones_col", [128, 1], BF16)
    ones_row = sb("ones_row", [1, 128], BF16)
    n128_row = sb("n128_row", [1, 128], BF16)
    cdb = sb("cdb_s", [128, 32], F32)
    ascl = sb("ascl", [128, 16], F32)
    abr = sb("abr", [128, 16], F32)
    gsc = sb("gsc_s", [128, 64], F32)
    usc = sb("usc_s", [128, 64], F32)
    dsc = sb("dsc_s", [128, 16], F32)
    st0 = sb("st0", [128, 16], F32)
    st1 = sb("st1", [128, 16], F32)

    bufA32 = bufA[:, :].bitcast(F32)
    bufB32 = bufB[:, :].bitcast(F32)

    pb = [es.enter_context(nc.psum_tensor(f"pb{k}", [128, T], F32)) for k in range(8)]

    sems = {nm: es.enter_context(nc.semaphore("sem_" + nm))
            for nm in ["dx", "pe", "v", "a", "g"] + [f"dw{i}" for i in range(Build.N_WLANES)]}

    b = Build(nc, sems)
    banks = Banks(b, pb)
    ring = Ring(b, ring_t, 8)

    def A32(c, lo=0, w=T):
        return bufA32[:, c * T + lo: c * T + lo + w]

    def B32(c, lo=0, w=T):
        return bufB32[:, c * T + lo: c * T + lo + w]

    def XB(c, lo=0, w=T):
        return xb[:, c * T + lo: c * T + lo + w]

    def XN(c, lo=0, w=T):
        return xn[:, c * T + lo: c * T + lo + w]

    def h_ap(c, lo=0, w=T):
        if c < 32:
            t, off = bufB, c * T
        elif c < 48:
            t, off = ro, (c - 32) * T
        else:
            t, off = loc, (c - 48) * T
        return t[:, off + lo: off + lo + w]

    st = {}

    def prologue():
        b.op("gpsimd", lambda e: e.memset(ones_col[:, :], 1.0), inc="g")
        b.op("gpsimd", lambda e: e.memset(ones_row[:, :], 1.0), inc="g")
        b.op("gpsimd", lambda e: e.memset(n128_row[:, :], -128.0), inc="g")
        for nm, dst in [("cdb", cdb), ("gsc", gsc), ("usc", usc), ("dsc", dsc),
                        ("inist", st0), ("logd", ascl)]:
            b.sdma(lambda e, d=dst, s=dr[nm]: e.dma_start(d[:, :], s))
        b.sdma(lambda e: e.dma_start(h32[:, 0:1024], dr["amat"]))
        pro_dx = b.sdma(lambda e: e.dma_start(h32[:, 1024:2048], dr["bmat"]))
        st["pro_dx"] = pro_dx
        b.wait("vector", "dx", pro_dx)
        b.wait("scalar", "dx", pro_dx)
        b.op("vector", lambda e: e.memset(st1[:, :], 0.0))
        b.op("vector", lambda e: e.tensor_mul(sp[:, 0:1024], h32[:, 0:1024], h32[:, 1024:2048]))
        b.drain("vector")
        for c in range(16):
            b.op("vector", lambda e, c=c: e.tensor_reduce(
                abr[:, c:c + 1], sp[:, c * 64:(c + 1) * 64], mybir.AxisListType.X, ALU.add))
        b.drain("vector")
        tv = b.op("vector", lambda e: e.tensor_single_scalar(sp[:, 16:32], ascl[:, :], 0.0, ALU.max), inc="v")
        b.wait("scalar", "v", tv)
        b.op("scalar", lambda e: e.activation(sp[:, 0:16], ascl[:, :], AF.Abs))
        b.drain("scalar")
        b.op("scalar", lambda e: e.activation(sp[:, 32:48], sp[:, 0:16], AF.Exp, scale=-1.0))
        b.drain("scalar")
        ta = b.op("scalar", lambda e: e.activation(sp[:, 48:64], sp[:, 32:48], AF.Ln, bias=1.0), inc="a")
        b.wait("vector", "a", ta)
        tv = b.op("vector", lambda e: e.tensor_add(sp[:, 64:80], sp[:, 16:32], sp[:, 48:64]), inc="v")
        b.wait("scalar", "v", tv)
        st["prologue_a"] = b.op("scalar", lambda e: e.activation(ascl[:, :], sp[:, 64:80], AF.Exp, scale=-1.0), inc="a")

    def load_x(tile):
        b.wait("sync", "v", st.get("combine_v", 0))
        b.wait("sync", "dx", st.get("out_dx", 0))
        src = xT.rearrange("(c p) s -> p c s", p=128)[:, :, tile * T:(tile + 1) * T]
        dst = xb[:, :].rearrange("p (c t) -> p c t", c=HC)
        st["xb_dx"] = b.sdma(lambda e, d=dst, s=src: e.dma_start(d, s))

    def layernorm(layer):
        pre = st.pop("ln_pre", None)
        b.wait("scalar", "dx", st["xb_dx"])
        b.wait("scalar", "v", st.get("combine_v", 0))
        if pre is None:
            b.wait("scalar", "pe", st.get("xn_free_pe", 0))
            b.op("scalar", lambda e: e.activation(bufA[:, 0:HC * T], xb[:, :], AF.Copy))
            ta_sq = b.op("scalar", lambda e: e.activation(xn[:, :], xb[:, :], AF.Square), inc="a")
            bm = banks.alloc()
            bs = banks.alloc()
            b.wait("tensor", "a", ta_sq)
            b.wait("tensor", "g", 3)
            for c in range(HC):
                b.op("tensor", lambda e, c=c, k=bm: e.matmul(
                    pb[k][0:1, :], ones_col[:, :], bufA[:, c * T:(c + 1) * T],
                    start=(c == 0), stop=(c == HC - 1)),
                    inc="pe" if c == HC - 1 else None)
            for c in range(HC):
                b.op("tensor", lambda e, c=c, k=bs: e.matmul(
                    pb[k][0:1, :], ones_col[:, :], XN(c),
                    start=(c == 0), stop=(c == HC - 1)),
                    inc="pe" if c == HC - 1 else None)
            t_ps = b.cnt["pe"]
        else:
            bm, bs, t_ps = pre
        SL = lambda i: sp[0:1, i * T:(i + 1) * T]
        b.wait("vector", "pe", t_ps)
        b.op("vector", lambda e: e.tensor_single_scalar(SL(0), pb[bm][0:1, :], 1.0 / H, ALU.mult))
        b.op("vector", lambda e: e.tensor_single_scalar(SL(1), pb[bs][0:1, :], 1.0 / H, ALU.mult))
        b.drain("vector")
        b.op("vector", lambda e: e.tensor_mul(SL(2), SL(0), SL(0)))
        b.drain("vector")
        b.op("vector", lambda e: e.tensor_sub(SL(1), SL(1), SL(2)))
        b.drain("vector")
        tv = b.op("vector", lambda e: e.tensor_single_scalar(SL(1), SL(1), EPS, ALU.add), inc="v")
        banks.release(bm, "v", tv)
        banks.release(bs, "v", tv)
        b.wait("scalar", "v", tv)
        b.op("scalar", lambda e: e.activation(SL(2), SL(1), AF.Ln))
        b.drain("scalar")
        b.op("scalar", lambda e: e.activation(SL(1), SL(2), AF.Exp, scale=-0.5))
        b.drain("scalar")
        b.op("scalar", lambda e: e.activation(stat_bf[0:1, 0:T], SL(0), AF.Copy))
        ta = b.op("scalar", lambda e: e.activation(stat_bf[0:1, T:2 * T], SL(1), AF.Copy), inc="a")
        ba = banks.alloc()
        bb = banks.alloc()
        b.wait("tensor", "a", ta)
        b.op("tensor", lambda e, k=ba: e.matmul(pb[k][:, :], ones_row[:, :], stat_bf[0:1, 0:T],
                                                start=True, stop=True))
        t_bc = b.op("tensor", lambda e, k=bb: e.matmul(pb[k][:, :], ones_row[:, :], stat_bf[0:1, T:2 * T],
                                                       start=True, stop=True), inc="pe")
        b.wait("scalar", "pe", t_bc)
        b.op("scalar", lambda e, k=ba: e.activation(tg[:, 0:T], pb[k][:, :], AF.Copy))
        ta2 = b.op("scalar", lambda e, k=bb: e.activation(tg2[:, 0:T], pb[k][:, :], AF.Copy), inc="a")
        banks.release(ba, "a", ta2)
        banks.release(bb, "a", ta2)
        NHALF = HC // 2
        xv = lambda lohalf: xb[:, lohalf * NHALF * T:(lohalf + 1) * NHALF * T].rearrange(
            "p (c t) -> p c t", c=NHALF)
        bv = lambda lohalf: bufB32[:, lohalf * NHALF * T:(lohalf + 1) * NHALF * T].rearrange(
            "p (c t) -> p c t", c=NHALF)
        nv = lambda lohalf: xn[:, lohalf * NHALF * T:(lohalf + 1) * NHALF * T].rearrange(
            "p (c t) -> p c t", c=NHALF)
        mbc = tg[:, 0:T].unsqueeze(1).broadcast_to([128, NHALF, T])
        rbc = tg2[:, 0:T].unsqueeze(1).broadcast_to([128, NHALF, T])
        b.wait("vector", "a", ta2)
        b.wait("vector", "dx", st["xb_dx"])
        b.wait("vector", "v", st.get("combine_v", 0))
        b.op("vector", lambda e: e.tensor_sub(bv(0), xv(0), mbc))
        b.drain("vector")
        st["xn_v"] = b.op("vector", lambda e: e.tensor_mul(nv(0), bv(0), rbc), inc="v")
        b.wait("gpsimd", "a", ta2)
        b.wait("gpsimd", "pe", t_bc)
        b.wait("gpsimd", "v", st.get("combine_v", 0))
        b.wait("gpsimd", "dx", st["xb_dx"])
        b.op("gpsimd", lambda e: e.tensor_sub(bv(1), xv(1), mbc))
        b.drain("gpsimd")
        st["xn_g"] = b.op("gpsimd", lambda e: e.tensor_mul(nv(1), bv(1), rbc), inc="g")
        b.drain("gpsimd")
        b.drain("vector")

    def mm_family(stream, u, bank, rhs_fn, n_k, w=T, extra=None, consume=True):
        base, dw_tok = stream.unit(u)
        b.wait("tensor", *dw_tok)
        b.wait("tensor", "v", st.get("xn_v", 0))
        b.wait("tensor", "g", st.get("xn_g", 0))
        out = pb[bank][:, 0:w]
        for k in range(n_k):
            last = (k == n_k - 1 and extra is None)
            b.op("tensor", lambda e, k=k, out=out, base=base, last=last: e.matmul(
                out, stream.lhsT(base, k), rhs_fn(k), start=(k == 0), stop=last),
                inc="pe" if last else None)
        if extra is not None:
            b.op("tensor", lambda e, out=out: e.matmul(out, extra[0], extra[1],
                                                       start=False, stop=True), inc="pe")
        t = b.cnt["pe"]
        if consume:
            stream.done(u, t)
        return t

    def fiv(streams):
        s_f, s_i, s_v = streams["wf"], streams["wi"], streams["wv"]
        for m in range(HC):
            k = banks.alloc()
            t = mm_family(s_f, m, k, lambda kk: XN(kk), HC)
            b.wait("scalar", "pe", t)
            ta = b.op("scalar", lambda e, k=k, m=m: e.activation(A32(m), pb[k][:, :], AF.Tanh), inc="a")
            banks.release(k, "a", ta)
        st["f_a"] = ta
        gp = {}
        for m in range(HC):
            ki = banks.alloc()
            ti = mm_family(s_i, m, ki, lambda kk: XN(kk), HC)
            kv = banks.alloc()
            tvm = mm_family(s_v, m, kv, lambda kk: XN(kk), HC)
            sl = (m % 2) * T
            b.wait("scalar", "v", gp.get(m - 2, 0))
            b.wait("scalar", "pe", ti)
            b.op("scalar", lambda e, k=ki, sl=sl: e.activation(tg[:, sl:sl + T], pb[k][:, :], AF.Sigmoid))
            b.wait("scalar", "pe", tvm)
            ta = b.op("scalar", lambda e, k=kv, sl=sl: e.activation(tg2[:, sl:sl + T], pb[k][:, :], AF.Silu), inc="a")
            banks.release(ki, "a", ta)
            banks.release(kv, "a", ta)
            b.wait("vector", "a", ta)
            gp[m] = b.op("vector", lambda e, m=m, sl=sl: e.tensor_mul(B32(m), tg[:, sl:sl + T], tg2[:, sl:sl + T]), inc="v")
        st["g_v"] = gp[HC - 1]
        st["fiv_a"] = ta

    def scan0():
        b.wait("vector", "a", st["f_a"])
        b.wait("vector", "dx", st["pro_dx"])
        b.drain("vector")
        for c in range(HC):
            b.op("vector", lambda e, c=c: e.tensor_tensor_scan(
                B32(c), A32(c), B32(c), st0[:, c:c + 1], ALU.mult, ALU.add))
        b.drain("vector")
        for c in range(HC):
            last = (c == HC - 1)
            b.op("vector", lambda e, c=c: e.tensor_copy(st0[:, c:c + 1], B32(c, T - 1, 1)),
                 inc="v" if last else None)
        st["scan_v"] = b.cnt["v"]
        b.drain("vector")

    def local(streams):
        s_u, s_g = streams["wup"], streams["wgl"]
        b.wait("scalar", "v", st["g_v"])
        lv = {}
        for m in range(HC):
            ku = banks.alloc()
            tu = mm_family(s_u, m, ku, lambda kk: XN(kk), HC)
            kg = banks.alloc()
            tgm = mm_family(s_g, m, kg, lambda kk: XN(kk), HC)
            sl = (m % 2) * T
            b.wait("scalar", "pe", tgm)
            b.wait("scalar", "v", lv.get(m - 2, 0))
            ta = b.op("scalar", lambda e, k=kg, sl=sl: e.activation(tg2[:, sl:sl + T], pb[k][:, :], AF.Silu), inc="a")
            banks.release(kg, "a", ta)
            b.wait("vector", "a", ta)
            b.wait("vector", "pe", tu)
            lv[m] = b.op("vector", lambda e, k=ku, m=m, sl=sl: e.tensor_mul(
                loc[:, m * T:(m + 1) * T], pb[k][:, :], tg2[:, sl:sl + T]), inc="v")
            banks.release(ku, "v", lv[m])
        st["loc_done_v"] = lv[HC - 1]

    def q_readout(streams):
        s_q = streams["wq"]
        b.wait("vector", "a", st["fiv_a"])
        ra = {}
        for m in range(HC):
            k = banks.alloc()
            t = mm_family(s_q, m, k, lambda kk: XN(kk), HC)
            sl = (m % 2) * T
            b.wait("vector", "pe", t)
            b.wait("vector", "v", st["scan_v"])
            b.wait("vector", "a", ra.get(m - 2, 0))
            tvp = b.op("vector", lambda e, k=k, m=m, sl=sl: e.tensor_mul(tg[:, sl:sl + T], pb[k][:, :], B32(m)), inc="v")
            banks.release(k, "v", tvp)
            b.wait("scalar", "v", tvp)
            ra[m] = b.op("scalar", lambda e, m=m, sl=sl: e.activation(
                ro[:, m * T:(m + 1) * T], tg[:, sl:sl + T], AF.Silu), inc="a")
        st["ro_done_a"] = ra[HC - 1]
        st["q_v"] = b.cnt["v"]

    def saccum(streams):
        s_ro, s_dn = streams["wro"], streams["wdnl"]
        for m in range(HC):
            k = banks.alloc()
            base_r, dw_r = s_ro.unit(m)
            base_d, dw_d = s_dn.unit(m)
            b.wait("tensor", *dw_r)
            b.wait("tensor", *dw_d)
            b.wait("tensor", "a", st["ro_done_a"])
            b.wait("tensor", "v", st["loc_done_v"])
            for kk in range(HC):
                b.op("tensor", lambda e, kk=kk, base=base_r, k=k: e.matmul(
                    pb[k][:, :], s_ro.lhsT(base, kk), ro[:, kk * T:(kk + 1) * T],
                    start=(kk == 0), stop=False))
            for kk in range(HC):
                b.op("tensor", lambda e, kk=kk, base=base_d, k=k: e.matmul(
                    pb[k][:, :], s_dn.lhsT(base, kk), loc[:, kk * T:(kk + 1) * T],
                    start=False, stop=(kk == HC - 1)),
                    inc="pe" if kk == HC - 1 else None)
            t = b.cnt["pe"]
            s_ro.done(m, t)
            s_dn.done(m, t)
            b.wait("scalar", "pe", t)
            b.wait("scalar", "v", st["scan_v"])
            ta = b.op("scalar", lambda e, k=k, m=m: e.activation(A32(m), pb[k][:, :], AF.Copy), inc="a")
            banks.release(k, "a", ta)
        st["s_sb_a"] = ta
        st["h_free_pe"] = t

    def zrow(slot):
        k = banks.alloc()
        b.wait("tensor", "v", st["xn_v"])
        for c in range(HC):
            b.op("tensor", lambda e, c=c, k=k: e.matmul(
                pb[k][0:1, :], ones_col[:, :], XN(c), start=(c == 0), stop=(c == HC - 1)),
                inc="pe" if c == HC - 1 else None)
        t = b.cnt["pe"]
        b.wait("scalar", "pe", t)
        ta = b.op("scalar", lambda e, k=k, slot=slot: e.activation(
            stat_bf[0:1, slot * T:(slot + 1) * T], pb[k][0:1, :], AF.Copy), inc="a")
        banks.release(k, "a", ta)
        return ta

    def sparsify(gh, t_h, inc_last=False):
        A_ = lambda j: sp[:, j * TS2:(j + 1) * TS2]
        T_ = lambda k: sp[:, (4 + k) * TS2:(5 + k) * TS2]
        H_ = lambda j, s: h32[:, j * T + s * TS2: j * T + s * TS2 + TS2]
        out_t = None
        for s in (0, 1):
            b.drain("vector")
            for j in range(4):
                b.op("vector", lambda e, j=j, s=s: e.tensor_single_scalar(
                    A_(j), H_(j, s), -1.0, ALU.mult))
            b.drain("vector")
            for j in range(4):
                b.op("vector", lambda e, j=j, s=s: e.tensor_tensor(
                    A_(j), A_(j), H_(j, s), ALU.max))
            b.drain("vector")
            b.op("vector", lambda e: e.tensor_max(T_(0), A_(0), A_(1)))
            b.op("vector", lambda e: e.tensor_tensor(T_(1), A_(0), A_(1), ALU.min))
            b.op("vector", lambda e: e.tensor_max(T_(2), A_(2), A_(3)))
            b.op("vector", lambda e: e.tensor_tensor(T_(3), A_(2), A_(3), ALU.min))
            b.drain("vector")
            b.op("vector", lambda e: e.tensor_tensor(T_(0), T_(0), T_(2), ALU.min))
            b.op("vector", lambda e: e.tensor_max(T_(1), T_(1), T_(3)))
            b.drain("vector")
            b.op("vector", lambda e: e.tensor_max(T_(0), T_(0), T_(1)))
            b.drain("vector")
            for j in range(4):
                b.op("vector", lambda e, j=j: e.tensor_tensor(A_(j), A_(j), T_(0), ALU.is_ge))
            b.drain("vector")
            for j in range(4):
                c = 16 * j + gh
                last = (s == 1 and j == 3)
                out_t = b.op("vector", lambda e, j=j, s=s, c=c: e.tensor_mul(
                    h_ap(c, s * TS2, TS2), A_(j), H_(j, s)), inc="v" if last else None)
        st["sp_free_v"] = out_t
        b.drain("vector")
        return out_t

    def ffnA(layer, streams):
        quant = (layer == 1)
        s_g = streams["f1g" if quant else "f0g"]
        s_u = streams["f1u" if quant else "f0u"]
        if quant:
            b.wait("scalar", "v", st["cd_v"])
        else:
            b.wait("scalar", "v", st["q_v"])
        order = list(range(IC)) if quant else _L0_ORDER
        b.wait("vector", "pe", st.get("h_free_pe", 0))
        b.drain("vector")
        hp = {}
        tvh = None
        for idx, m in enumerate(order):
            kg = banks.alloc()
            tg_t = mm_family(s_g, idx, kg, lambda kk: XN(kk), HC)
            ku = banks.alloc()
            tu_t = mm_family(s_u, idx, ku, lambda kk: XN(kk), HC)
            sl = (idx % 2) * T
            b.wait("scalar", "pe", tg_t)
            b.wait("scalar", "v", hp.get(idx - 2, 0))
            if quant:
                ta = b.op("scalar", lambda e, k=kg, m=m, sl=sl: e.activation(
                    tg[:, sl:sl + T], pb[k][:, :], AF.Silu, scale=gsc[:, m:m + 1]), inc="a")
            else:
                ta = b.op("scalar", lambda e, k=kg, sl=sl: e.activation(
                    tg[:, sl:sl + T], pb[k][:, :], AF.Silu), inc="a")
            banks.release(kg, "a", ta)
            b.wait("vector", "a", ta)
            b.wait("vector", "pe", tu_t)
            if quant:
                hp[idx] = b.op("vector", lambda e, k=ku, m=m, sl=sl: e.scalar_tensor_tensor(
                    h_ap(m), pb[k][:, :], usc[:, m:m + 1], tg[:, sl:sl + T],
                    ALU.mult, ALU.mult), inc="v")
            else:
                j = m // 16
                hp[idx] = b.op("vector", lambda e, k=ku, j=j, sl=sl: e.tensor_mul(
                    h32[:, j * T:(j + 1) * T], pb[k][:, :], tg[:, sl:sl + T]), inc="v")
            banks.release(ku, "v", hp[idx])
            if not quant and m >= 48:
                tvh = sparsify(m - 48, hp[idx], inc_last=(m == 63))
        st["h_v"] = tvh if not quant else hp[IC - 1]
        st["xn_last_pe"] = tu_t

    def ffnB(layer, streams, tile=0):
        quant = (layer == 1)
        stat = (layer == 0) or (tile + 1 < n_tiles)
        xtok = None
        if quant and stat:
            b.drain("gpsimd")
            b.wait("gpsimd", "pe", st["xn_last_pe"])
            srcx = xT.rearrange("(c p) s -> p c s", p=128)[:, :, (tile + 1) * T:(tile + 2) * T]
            dstx = xn[:, :].rearrange("p (c t) -> p c t", c=HC)
            xtok = b.wdma(lambda e, d=dstx, s=srcx: e.dma_start(d, s))
        s_d = streams["f1d" if quant else "f0d"]
        zh_a = 0
        if quant:
            k = banks.alloc()
            b.wait("tensor", "v", st["h_v"])
            for c in range(IC):
                b.op("tensor", lambda e, c=c, k=k: e.matmul(
                    pb[k][0:1, :], ones_col[:, :], h_ap(c), start=(c == 0), stop=(c == IC - 1)),
                    inc="pe" if c == IC - 1 else None)
            t = b.cnt["pe"]
            b.wait("scalar", "pe", t)
            zh_a = b.op("scalar", lambda e, k=k: e.activation(
                stat_bf[0:1, T:2 * T], pb[k][0:1, :], AF.Copy), inc="a")
            banks.release(k, "a", zh_a)
        b.wait("scalar", "v", st["h_v"])
        cb = {}
        for m in range(HC):
            qu = [s_d.unit(4 * m + q) for q in range(4)]
            k = banks.alloc()
            for _, tok in qu:
                b.wait("tensor", *tok)
            b.wait("tensor", "v", st["h_v"])
            if quant:
                b.wait("tensor", "a", zh_a)
            for kk in range(IC):
                base = qu[kk // 16][0]
                ko = kk % 16
                last = (kk == IC - 1 and not quant)
                b.op("tensor", lambda e, kk=kk, base=base, ko=ko, k=k, last=last: e.matmul(
                    pb[k][:, :],
                    ring_t[:, base * 2048 + ko * 128: base * 2048 + (ko + 1) * 128],
                    h_ap(kk), start=(kk == 0), stop=last),
                    inc="pe" if last else None)
            if quant:
                b.op("tensor", lambda e, k=k: e.matmul(
                    pb[k][:, :], n128_row[:, :], stat_bf[0:1, T:2 * T],
                    start=False, stop=True), inc="pe")
            t = b.cnt["pe"]
            sl = (m % 2) * T
            b.wait("scalar", "pe", t)
            b.wait("scalar", "v", cb.get(m - 2, 0))
            if quant:
                ta = b.op("scalar", lambda e, k=k, m=m, sl=sl: e.activation(
                    tg[:, sl:sl + T], pb[k][:, :], AF.Silu, scale=dsc[:, m:m + 1]), inc="a")
            else:
                ta = b.op("scalar", lambda e, k=k, sl=sl: e.activation(
                    tg[:, sl:sl + T], pb[k][:, :], AF.Silu), inc="a")
            banks.release(k, "a", ta)
            b.wait("vector", "a", ta)
            if layer == 0:
                b.wait("vector", "a", st["s_sb_a"])
            b.op("vector", lambda e, m=m, sl=sl: e.tensor_mul(
                tg2[:, sl:sl + T], tg[:, sl:sl + T], A32(m)))
            b.drain("vector")
            cb[m] = b.op("vector", lambda e, m=m, sl=sl: e.tensor_add(
                XB(m), XB(m), tg2[:, sl:sl + T]), inc="v")
            for q in range(4):
                s_d.done(4 * m + q, t)
            if stat:
                b.wait("scalar", "v", cb[m])
                if not quant:
                    b.wait("scalar", "pe", st["xn_last_pe"])
                    b.wait("scalar", "g", st.get("xn_g", 0))
                    b.op("scalar", lambda e, m=m: e.activation(XN(m), XB(m), AF.Copy))
                    ta_stat = b.op("scalar", lambda e, m=m: e.activation(
                        bufA[:, m * T:(m + 1) * T], XB(m), AF.Square),
                        inc="a" if m == HC - 1 else None)
                else:
                    b.wait("scalar", *xtok)
                    ta_stat = b.op("scalar", lambda e, m=m: e.activation(
                        bufA[:, m * T:(m + 1) * T], XN(m), AF.Square),
                        inc="a" if m == HC - 1 else None)
        st["combine_v"] = cb[HC - 1]
        st["h_free_pe"] = t
        if stat:
            bm = banks.alloc()
            bs = banks.alloc()
            b.wait("tensor", "a", ta_stat)
            if xtok is not None:
                b.wait("tensor", *xtok)
            for c in range(HC):
                b.op("tensor", lambda e, c=c, k=bm: e.matmul(
                    pb[k][0:1, :], ones_col[:, :], XN(c),
                    start=(c == 0), stop=(c == HC - 1)),
                    inc="pe" if c == HC - 1 else None)
            for c in range(HC):
                b.op("tensor", lambda e, c=c, k=bs: e.matmul(
                    pb[k][0:1, :], ones_col[:, :], bufA[:, c * T:(c + 1) * T],
                    start=(c == 0), stop=(c == HC - 1)),
                    inc="pe" if c == HC - 1 else None)
            st["ln_pre"] = (bm, bs, b.cnt["pe"])

    def cd(streams):
        s_c, s_dw = streams["cT"], streams["dT"]
        b.wait("scalar", "v", st["combine_v"])
        b.drain("vector")
        xv = {}
        for m in range(HC):
            kc = banks.alloc()
            tc = mm_family(s_c, m, kc, lambda kk: XN(kk), HC)
            kd = banks.alloc()
            td = mm_family(s_dw, m, kd, lambda kk: XN(kk), HC)
            sl = (m % 2) * T
            b.wait("scalar", "pe", td)
            b.wait("scalar", "v", xv.get(m - 2, 0))
            ta = b.op("scalar", lambda e, k=kd, m=m, sl=sl: e.activation(
                tg[:, sl:sl + T], pb[k][:, :], AF.Sigmoid, bias=cdb[:, 16 + m:17 + m]), inc="a")
            banks.release(kd, "a", ta)
            b.wait("vector", "a", ta)
            b.wait("vector", "pe", tc)
            xv[m] = b.op("vector", lambda e, k=kc, m=m, sl=sl: e.scalar_tensor_tensor(
                A32(m), pb[k][:, :], cdb[:, m:m + 1], tg[:, sl:sl + T], ALU.add, ALU.mult), inc="v")
            banks.release(kc, "v", xv[m])
        st["xp_done_v"] = xv[HC - 1]
        st["cd_v"] = xv[HC - 1]

    def scan1():
        b.wait("vector", "a", st["prologue_a"])
        b.drain("vector")
        for c in range(HC):
            b.op("vector", lambda e, c=c: e.tensor_tensor_scan(
                A32(c), ascl[:, c:c + 1].broadcast_to([128, T]), A32(c),
                st1[:, c:c + 1], ALU.mult, ALU.add))
        b.drain("vector")
        for c in range(HC):
            b.op("vector", lambda e, c=c: e.tensor_copy(st1[:, c:c + 1], A32(c, T - 1, 1)))
        b.drain("vector")
        for c in range(HC):
            last = (c == HC - 1)
            b.op("vector", lambda e, c=c: e.tensor_scalar_mul(A32(c), A32(c), abr[:, c:c + 1]),
                 inc="v" if last else None)
        b.drain("vector")
        st["s1_v"] = b.cnt["v"]

    def store_out(tile):
        b.wait("sync", "v", st["combine_v"])
        src = xb[:, :].rearrange("p (c t) -> p c t", c=HC)
        dst = yT.rearrange("(c p) s -> p c s", p=128)[:, :, tile * T:(tile + 1) * T]
        st["out_dx"] = b.sdma(lambda e, d=dst, s=src: e.dma_start(d, s))

    prologue()

    def mk_streams():
        cfg = [("wf", 16, 2048, 1, 2), ("wi", 16, 2048, 1, 2), ("wv", 16, 2048, 1, 2),
               ("wq", 16, 2048, 1, 2), ("wro", 16, 2048, 1, 1), ("wup", 16, 2048, 1, 2),
               ("wgl", 16, 2048, 1, 2), ("wdnl", 16, 2048, 1, 1),
               ("f0g", 64, 2048, 1, 2), ("f0u", 64, 2048, 1, 2), ("f0d", 64, 2048, 1, 2),
               ("cT", 16, 2048, 1, 2), ("dT", 16, 2048, 1, 2),
               ("f1g", 64, 2048, 1, 2), ("f1u", 64, 2048, 1, 2), ("f1d", 64, 2048, 1, 2)]
        return {nm: PanelStream(b, ring, dr[nm], n, ue, spu, grp)
                for nm, n, ue, spu, grp in cfg}

    for tile in range(n_tiles):
        streams = mk_streams()
        load_x(tile)
        st["xn_free_pe"] = st.get("h_free_pe", 0)
        layernorm(0)
        fiv(streams)
        scan0()
        local(streams)
        q_readout(streams)
        saccum(streams)
        ffnA(0, streams)
        ffnB(0, streams, tile)
        st["xn_free_pe"] = st["h_free_pe"]
        layernorm(1)
        cd(streams)
        scan1()
        ffnA(1, streams)
        ffnB(1, streams, tile)
        store_out(tile)

    b.wait("sync", "dx", b.cnt["dx"])

    with es, nc.Block() as block:
        @block.tensor
        def _(eng):
            for fn in b.ops["tensor"]:
                fn(eng)

        @block.vector
        def _(eng):
            for fn in b.ops["vector"]:
                fn(eng)

        @block.scalar
        def _(eng):
            for fn in b.ops["scalar"]:
                fn(eng)

        @block.gpsimd
        def _(eng):
            for fn in b.ops["gpsimd"]:
                fn(eng)

        @block.sync
        def _(eng):
            for fn in b.ops["sync"]:
                fn(eng)

    return nc




def kernel(**inputs):
    w = host_prep(inputs)
    xs = np.asarray(inputs["L0_attn_x"])
    nc = build_nc()
    in_maps = []
    for bidx in range(NCORES):
        m = dict(w)
        m["xT"] = np.ascontiguousarray(xs[bidx].T)
        in_maps.append(m)
    res = run_bass_kernel_spmd(nc, in_maps, core_ids=list(range(NCORES)))
    out = np.stack([np.asarray(res.results[bidx]["yT"]).T for bidx in range(NCORES)])
    return out.astype(np.float32)
